# version 08 (final)
# speedup vs baseline: 1.2920x; 1.0035x over previous
import contextlib
import numpy as np
import ml_dtypes
import concourse.bass as bass
import concourse.mybir as mybir
from concourse.bass_utils import run_bass_kernel_spmd

F32 = mybir.dt.float32
BF16 = mybir.dt.bfloat16
I32 = mybir.dt.int32
F16 = mybir.dt.float16
AF = mybir.ActivationFunctionType
ALU = mybir.AluOpType
BF = ml_dtypes.bfloat16

D = 2048
NT = 2048
NH, HD = 32, 64
SEQ = 8192
I_FFN, GUP = 5632, 11264
AH, ADH = 16, 128
NE, EHID = 16, 1024
LN_EPS, RMS_EPS, TOPK_EPS = 1e-5, 1e-6, 1e-6

ENGINES = ("tensor", "vector", "scalar", "gpsimd", "sync")
_N_LAUNCH_MODE = "unfused"
DBG_OUTS = False


class Prog:
    def __init__(self, nc, block, sems):
        self.nc = nc
        self.block = block
        self.sems = sems
        self.counts = {k: 0 for k in sems}
        self.state = {}
        self.waited = {e: {} for e in ENGINES}

    def _need(self, eng, sem, val):
        if val <= 0:
            return False
        if sem == "tensor" and eng == "tensor":
            return False
        return self.waited[eng].get(sem, 0) < val

    def _waits_for(self, reads, writes):
        waits = {}

        def add(tok):
            s, v = tok
            if v > waits.get(s, 0):
                waits[s] = v

        for k in reads:
            st = self.state.get(k)
            if st and st["w"]:
                add(st["w"])
        for k in writes:
            st = self.state.get(k)
            if st:
                if st["w"]:
                    add(st["w"])
                for r in st["r"]:
                    add(r)
        return waits

    def op(self, engine, emit, reads=(), writes=(), dma=False, tag=""):
        waits = self._waits_for(reads, writes)
        sem_name = engine + "_dma" if dma else engine
        if dma:
            prev = self.counts[sem_name]
            if prev > waits.get(sem_name, 0):
                waits[sem_name] = prev
        waits = {s: v for s, v in waits.items() if self._need(engine, s, v)}
        amount = 16 if dma else 1
        self.counts[sem_name] += amount
        myval = self.counts[sem_name]
        sems = self.sems

        def body(eng):
            for s, v in sorted(waits.items(), key=lambda kv: str(kv[0])):
                eng.wait_ge(sems[s], v)
            emit(eng).then_inc(sems[sem_name], amount)

        getattr(self.block, engine)(body)
        for s, v in waits.items():
            if v > self.waited[engine].get(s, 0):
                self.waited[engine][s] = v
        tok = (sem_name, myval)
        for k in reads:
            self.state.setdefault(k, {"w": None, "r": []})["r"].append(tok)
        for k in writes:
            self.state[k] = {"w": tok, "r": []}
        return tok

    def join_all(self, include_cc=False):
        tgt = {s: v for s, v in self.counts.items()
               if include_cc or not (isinstance(s, tuple) and s[0] == "cc")}
        sems = self.sems
        for e in ENGINES:
            waits = {s: v for s, v in tgt.items() if self._need(e, s, v)}

            def body(eng, waits=waits):
                for s, v in sorted(waits.items(), key=lambda kv: str(kv[0])):
                    eng.wait_ge(sems[s], v)

            getattr(self.block, e)(body)
            for s, v in waits.items():
                self.waited[e][s] = max(self.waited[e].get(s, 0), v)

    def collective(self, kind, op, groups, in_ap, out_ap, reads=(), writes=()):
        sem_name = ("cc", self._cc_next)
        self._cc_next += 1
        assert sem_name in self.sems, "cc sem pool exhausted"
        waits = self._waits_for(reads, writes)
        waits = {s: v for s, v in waits.items() if self._need("gpsimd", s, v)}
        self.counts[sem_name] = 1
        sems = self.sems

        def body(eng):
            for s, v in sorted(waits.items(), key=lambda kv: str(kv[0])):
                eng.wait_ge(sems[s], v)
            eng.collective_compute(kind, op, replica_groups=groups,
                                   ins=[in_ap.opt()],
                                   outs=[out_ap.opt()]).then_inc(
                                       sems[sem_name], 1)

        self.block.gpsimd(body)
        for s, v in waits.items():
            self.waited["gpsimd"][s] = max(self.waited["gpsimd"].get(s, 0), v)
        tok = (sem_name, 1)
        for k in reads:
            self.state.setdefault(k, {"w": None, "r": []})["r"].append(tok)
        for k in writes:
            self.state[k] = {"w": tok, "r": []}
        return tok

    def wait_token(self, tok, engines=("sync", "scalar", "gpsimd", "tensor",
                                       "vector")):
        s, v = tok
        sems = self.sems
        for e in engines:
            if self._need(e, s, v):
                def body(eng, s=s, v=v):
                    eng.wait_ge(sems[s], v)

                getattr(self.block, e)(body)
                self.waited[e][s] = max(self.waited[e].get(s, 0), v)

    def finish(self):
        self.join_all(include_cc=True)


def cdiv(a, b):
    return (a + b - 1) // b


class Ctx:

    def __init__(self, nc, stack):
        self.nc = nc
        self.scopes = [stack]
        sems = {}
        for e in ENGINES:
            sems[e] = stack.enter_context(nc.semaphore("sem_" + e))
            sems[e + "_dma"] = stack.enter_context(nc.semaphore("semd_" + e))
        for i in range(16):
            sems[("cc", i)] = stack.enter_context(nc.semaphore(f"sem_cc{i}"))
        block = stack.enter_context(nc.Block())
        self.p = Prog(nc, block, sems)
        self.p._cc_next = 0
        self._uid = 0

    @contextlib.contextmanager
    def scope(self):
        with contextlib.ExitStack() as st:
            self.scopes.append(st)
            try:
                yield
            finally:
                self.scopes.pop()

    def sb(self, name, shape, dtype):
        self._uid += 1
        return self.scopes[-1].enter_context(
            self.nc.sbuf_tensor(f"{name}_u{self._uid}", shape, dtype))

    def ps(self, name, shape, dtype=F32):
        self._uid += 1
        return self.scopes[-1].enter_context(
            self.nc.psum_tensor(f"{name}_u{self._uid}", shape, dtype))

    def uid(self):
        self._uid += 1
        return self._uid

    def constf(self, val):
        if not hasattr(self, "_consts"):
            self._consts = {}
        if val not in self._consts:
            t = self.scopes[0].enter_context(
                self.nc.sbuf_tensor(f"const_{len(self._consts)}", [128, 1], F32))
            key = ("const", val)
            self.p.op("gpsimd", lambda e: e.memset(t[:], float(val)),
                      writes=[key])
            self._consts[val] = (t, key)
        return self._consts[val]


def dma(p, queue, out_ap, in_ap, reads, writes, tag=""):
    return p.op(queue, lambda e: e.dma_start(out=out_ap, in_=in_ap),
                reads=reads, writes=writes, dma=True, tag=tag)


def build_identity(c, dtype=BF16):
    p = c.p
    idf = c.sb("ident_f32", [128, 128], F32)
    idb = c.sb("ident_bf16", [128, 128], BF16)
    p.op("gpsimd", lambda e: e.iota(idf[:], [[1, 128]], base=0, channel_multiplier=-1,
                                    allow_small_or_imprecise_dtypes=True),
         writes=["identf"])
    p.op("vector", lambda e: e.tensor_single_scalar(idf[:], idf[:], 0.0, ALU.is_equal),
         reads=["identf"], writes=["identf"])
    p.op("vector", lambda e: e.tensor_copy(idb[:], idf[:]),
         reads=["identf"], writes=["identb"])
    return idf, idb


def mm_chain(c, name, wt_dram, K, O, T, act_view, act_keys, evac,
             psum_tiles, wt_bufs, ob_group=2, tblk=512, queue="sync",
             o_start=0, o_stop=None, wt_elem_bytes=2, wt_deps=()):
    p = c.p
    nkb = K // 128
    if o_stop is None:
        o_stop = O
    obs = list(range(o_start // 128, cdiv(o_stop, 128)))
    ntc = cdiv(T, tblk)
    for gi, ob0 in enumerate(range(0, len(obs), ob_group)):
        grp = obs[ob0:ob0 + ob_group]
        sbuf = wt_bufs[gi % 2]
        skey = ("wtb", name, gi % 2)
        src = wt_dram[:, grp[0] * 128: (grp[-1] + 1) * 128]
        src = src.rearrange("(kb q) o -> q kb o", q=128)
        dst = sbuf[:, 0:nkb * len(grp) * 128].rearrange(
            "q (kb o) -> q kb o", kb=nkb)
        dma(p, queue, dst, src, reads=list(wt_deps), writes=[skey],
            tag=f"{name}w{gi}")
        for obi, ob in enumerate(grp):
            for kb in range(nkb):
                st_ap = sbuf[:, (kb * len(grp) + obi) * 128:
                             (kb * len(grp) + obi) * 128 + 128]
                for tc in range(ntc):
                    t0, t1 = tc * tblk, min((tc + 1) * tblk, T)
                    ps = psum_tiles[tc]
                    p.op("tensor",
                         lambda e, st=st_ap, ps=ps, kb=kb, t0=t0, t1=t1: e.matmul(
                             ps[:, 0:t1 - t0], st, act_view(kb, t0, t1),
                             start=(kb == 0), stop=(kb == nkb - 1)),
                         reads=[skey] + list(act_keys),
                         writes=[(name, "ps", tc)])
            for tc in range(ntc):
                t0, t1 = tc * tblk, min((tc + 1) * tblk, T)
                evac(ob, t0, t1, psum_tiles[tc], (name, "ps", tc))


def phase0(c, x_dram, xT0_dram, xnT_dram, idf, idb):
    p = c.p
    nc = c.nc
    xt = [c.sb(f"p0_xt{i}", [128, D], F32) for i in range(2)]
    xnt = c.sb("p0_xnt", [128, D], BF16)
    sq = c.sb("p0_sq", [128, D], BF16)
    stats = c.sb("p0_stats", [128, 4], F32)
    tpsA = c.ps("p0_psA", [128, 128], F32)
    tpsB = c.ps("p0_psB", [128, 128], F32)
    tpsAb = c.ps("p0_psAb", [128, 128], F32)
    tpsBb = c.ps("p0_psBb", [128, 128], F32)
    xT_sb = c.sb("p0_xTsb", [128, 16 * 1024], F32)
    xnT_sb = c.sb("p0_xnTsb", [128, 16 * 1024], BF16)

    for half in range(2):
        for ti in range(8):
            tau = half * 8 + ti
            buf = xt[tau % 2]
            bk = ("p0xt", tau % 2)
            dma(p, "scalar", buf[:], x_dram[tau * 128:(tau + 1) * 128, :],
                reads=[], writes=[bk], tag=f"x{tau}")
            p.op("vector", lambda e, b=buf: e.tensor_reduce(
                stats[:, 0:1], b[:], axis=mybir.AxisListType.X, op=ALU.add),
                reads=[bk], writes=["st0"])
            p.op("scalar", lambda e, b=buf: e.activation(
                sq[:], b[:], AF.Square, accum_out=stats[:, 1:2]),
                reads=[bk], writes=["p0sq", "st1"])
            p.op("vector", lambda e: e.tensor_scalar(
                stats[:, 2:3], stats[:, 0:1], 1.0 / D, None, ALU.mult),
                reads=["st0"], writes=["st2"])
            p.op("vector", lambda e: e.tensor_tensor(
                stats[:, 0:1], stats[:, 2:3], stats[:, 2:3], ALU.mult),
                reads=["st2"], writes=["st0"])
            p.op("vector", lambda e: e.tensor_scalar(
                stats[:, 1:2], stats[:, 1:2], 1.0 / D, None, ALU.mult),
                reads=["st1"], writes=["st1"])
            p.op("vector", lambda e: e.tensor_tensor(
                stats[:, 1:2], stats[:, 1:2], stats[:, 0:1], ALU.subtract),
                reads=["st1", "st0"], writes=["st1"])
            ceps, cepsk = c.constf(LN_EPS)
            p.op("scalar", lambda e: e.activation(
                stats[:, 3:4], stats[:, 1:2], AF.Ln, bias=ceps[:, 0:1]),
                reads=["st1", cepsk], writes=["st3"])
            p.op("scalar", lambda e: e.activation(
                stats[:, 3:4], stats[:, 3:4], AF.Exp, scale=-0.5),
                reads=["st3"], writes=["st3"])
            p.op("vector", lambda e, b=buf: e.tensor_scalar(
                xnt[:], b[:], stats[:, 2:3], stats[:, 3:4],
                ALU.subtract, ALU.mult),
                reads=[bk, "st2", "st3"], writes=["p0xn"])
            for g in range(16):
                ps = tpsA if g % 2 == 0 else tpsB
                pk = ("p0ps", g % 2)
                p.op("tensor", lambda e, b=buf, g=g, ps=ps: e.matmul(
                    ps[:, 0:128], b[:, g * 128:(g + 1) * 128], idf[:],
                    start=True, stop=True),
                    reads=[bk, "identf"], writes=[pk])
                p.op("vector", lambda e, g=g, ti=ti, ps=ps: e.tensor_copy(
                    xT_sb[:, g * 1024 + ti * 128: g * 1024 + (ti + 1) * 128],
                    ps[:, 0:128]),
                    reads=[pk], writes=[("p0xTsb", g)])
            for g in range(16):
                ps = tpsAb if g % 2 == 0 else tpsBb
                pk = ("p0psb", g % 2)
                p.op("tensor", lambda e, g=g, ps=ps: e.matmul(
                    ps[:, 0:128], xnt[:, g * 128:(g + 1) * 128], idb[:],
                    start=True, stop=True),
                    reads=["p0xn", "identb"], writes=[pk])
                p.op("vector", lambda e, g=g, ti=ti, ps=ps: e.tensor_copy(
                    xnT_sb[:, g * 1024 + ti * 128: g * 1024 + (ti + 1) * 128],
                    ps[:, 0:128]),
                    reads=[pk], writes=[("p0xnTsb", g)])
        for g in range(16):
            dma(p, "sync", xT0_dram[g * 128:(g + 1) * 128,
                                    half * 1024:(half + 1) * 1024],
                xT_sb[:, g * 1024:(g + 1) * 1024],
                reads=[("p0xTsb", g)], writes=[("p0xT_out", g, half)])
            dma(p, "scalar", xnT_dram[g * 128:(g + 1) * 128,
                                      half * 1024:(half + 1) * 1024],
                xnT_sb[:, g * 1024:(g + 1) * 1024],
                reads=[("p0xnTsb", g)], writes=[("p0xnT_out", g, half)])
    p.join_all()


def load_actT(c, name, dram, sb, nbytes_per_group, groups=16, width=NT,
              queue="scalar", dtype_bytes=2):
    p = c.p
    keys = []
    for g in range(groups):
        k = (name, "g", g)
        dma(p, queue, sb[:, g * width:(g + 1) * width],
            dram[g * 128:(g + 1) * 128, :], reads=[], writes=[k])
        keys.append(k)
    return keys


def phase1(c, xnT_dram, WinT_dram, WdtT_dram, bdt_dram, xpT_dram, dt_dram,
           windep=(), wdtdep=()):
    p = c.p
    nc = c.nc
    act_sb = c.sb("p1_act", [128, 16 * NT], BF16)
    xp_sb = c.sb("p1_xp", [128, 16 * NT], BF16)
    wbufs = [c.sb(f"p1_wt{i}", [128, 16 * 2 * 128], BF16) for i in range(2)]
    pst = [c.ps(f"p1_ps{i}", [128, 512], F32) for i in range(4)]
    act_keys = load_actT(c, "p1act", xnT_dram, act_sb, 0)

    def act_view(kb, t0, t1):
        return act_sb[:, kb * NT + t0: kb * NT + t1]

    def evac(ob, t0, t1, ps, pskey):
        p.op("vector", lambda e: e.tensor_copy(
            xp_sb[:, ob * NT + t0: ob * NT + t1], ps[:, 0:t1 - t0]),
            reads=[pskey], writes=[("p1xp", ob)])

    mm_chain(c, "p1mm", WinT_dram.ap(), D, D, NT, act_view, act_keys, evac,
             pst, wbufs, wt_deps=windep)
    for g in range(16):
        dma(p, "gpsimd", xpT_dram[g * 128:(g + 1) * 128, :],
            xp_sb[:, g * NT:(g + 1) * NT], reads=[("p1xp", g)],
            writes=[("p1xpout", g)])

    wdt_sb = c.sb("p1_wdt", [128, 16 * 32], BF16)
    dma(p, "sync", wdt_sb[:].rearrange("q (kb o) -> q kb o", kb=16),
        WdtT_dram.ap().rearrange("(kb q) o -> q kb o", q=128),
        reads=list(wdtdep), writes=["wdt"])
    bdt_sb = c.sb("p1_bdt", [32, 1], F32)
    dma(p, "sync", bdt_sb[:], bdt_dram.ap(), reads=[], writes=["bdt"])
    dtps = [c.ps(f"p1_dtps{i}", [32, 512], F32) for i in range(2)]
    dt_sb = c.sb("p1_dt", [32, NT], F32)
    tmp = c.sb("p1_dttmp", [32, 512], F32)
    for tc in range(4):
        t0, t1 = tc * 512, (tc + 1) * 512
        for kb in range(16):
            p.op("tensor", lambda e, kb=kb, tc=tc, t0=t0, t1=t1: e.matmul(
                dtps[tc % 2][:, 0:512], wdt_sb[:, kb * 32:(kb + 1) * 32],
                xp_sb[:, kb * NT + t0: kb * NT + t1],
                start=(kb == 0), stop=(kb == 15)),
                reads=["wdt", ("p1xp", kb)], writes=[("dtps", tc % 2)])
        p.op("vector", lambda e, tc=tc, t0=t0: e.tensor_scalar(
            dt_sb[:, t0:t0 + 512], dtps[tc % 2][:, 0:512], bdt_sb[:], None,
            ALU.add),
            reads=[("dtps", tc % 2), "bdt"], writes=[("dtv", tc)])
        p.op("scalar", lambda e, t0=t0: e.activation(
            tmp[:, 0:512], dt_sb[:, t0:t0 + 512], AF.Abs),
            reads=[("dtv", tc)], writes=[("dta", tc)])
        p.op("scalar", lambda e: e.activation(
            tmp[:, 0:512], tmp[:, 0:512], AF.Exp, scale=-1.0),
            reads=[("dta", tc)], writes=[("dte", tc)])
        cone, conek = c.constf(1.0)
        p.op("scalar", lambda e: e.activation(
            tmp[:, 0:512], tmp[:, 0:512], AF.Ln, bias=cone[0:32, 0:1]),
            reads=[("dte", tc), conek], writes=[("dtl", tc)])
        p.op("scalar", lambda e, t0=t0: e.activation(
            dt_sb[:, t0:t0 + 512], dt_sb[:, t0:t0 + 512], AF.Relu),
            reads=[("dtv", tc)], writes=[("dtm", tc)])
        p.op("vector", lambda e, t0=t0: e.tensor_tensor(
            dt_sb[:, t0:t0 + 512], dt_sb[:, t0:t0 + 512], tmp[:, 0:512],
            ALU.add), reads=[("dtm", tc), ("dtl", tc)], writes=[("dtout", tc)])
    dma(p, "sync", dt_dram.ap(), dt_sb[:],
        reads=[("dtout", 0), ("dtout", 1), ("dtout", 2), ("dtout", 3)],
        writes=["dtdram"])
    p.join_all()


def load_acols(c, logA_dram):
    p = c.p
    acols = c.sb("acols", [128, 16], F32)
    dma(p, "sync", acols[:], logA_dram.ap(), reads=[], writes=["acols_raw"])
    p.op("scalar", lambda e: e.activation(acols[:], acols[:], AF.Exp),
         reads=["acols_raw"], writes=["acols_e"])
    p.op("vector", lambda e: e.tensor_scalar(acols[:], acols[:], -1.0, None,
                                             ALU.mult),
         reads=["acols_e"], writes=["acols"])
    return acols


def build_a_group(c, g, dt_dram, abuf, acols, tag):
    p = c.p
    k = ("abuf", tag)
    s0 = bass.AP(dt_dram, 2 * g * NT, [[0, 64], [1, NT]])
    s1 = bass.AP(dt_dram, (2 * g + 1) * NT, [[0, 64], [1, NT]])
    dma(p, "gpsimd", abuf[0:64, :], s0, reads=[], writes=[k])
    dma(p, "gpsimd", abuf[64:128, :], s1, reads=[k], writes=[k])
    p.op("scalar", lambda e: e.activation(
        abuf[:], abuf[:], AF.Exp, scale=acols[:, g:g + 1]),
        reads=[k, "acols"], writes=[k])
    return k


def scan_passA(c, xp_sb, xp_keys, dt_dram, logA_dram, sumP_ap, sumE_ap):
    p = c.p
    acols = load_acols(c, logA_dram)
    dtsum = c.sb("p2_dtsum", [32, 1], F32)
    dtsum_dram = c.nc.dram_tensor("p2_dtsum_dram", [32, 1], F32)
    dt_sb = c.sb("p2_dt", [32, NT], F32)
    dma(p, "sync", dt_sb[:], dt_dram.ap(), reads=[], writes=["p2dt"])
    p.op("vector", lambda e: e.tensor_reduce(
        dtsum[:], dt_sb[:], axis=mybir.AxisListType.X, op=ALU.add),
        reads=["p2dt"], writes=["p2dtsum"])
    dma(p, "sync", dtsum_dram.ap(), dtsum[:], reads=["p2dtsum"],
        writes=["p2dtsum_d"])
    ends = c.sb("p2_ends", [128, 16], F32)
    Ps = c.sb("p2_Ps", [128, 16], F32)
    dts_rep = c.sb("p2_dtsrep", [128, 16], F32)
    src = bass.AP(dtsum_dram, 0, [[1, 32], [0, 64]])
    for g in range(16):
        sg0 = bass.AP(dtsum_dram, 2 * g, [[0, 64], [1, 1]])
        sg1 = bass.AP(dtsum_dram, 2 * g + 1, [[0, 64], [1, 1]])
        dma(p, "gpsimd", dts_rep[0:64, g:g + 1], sg0,
            reads=["p2dtsum_d"], writes=[("dtsrep", g)])
        dma(p, "gpsimd", dts_rep[64:128, g:g + 1], sg1,
            reads=["p2dtsum_d", ("dtsrep", g)], writes=[("dtsrep", g)])
    p.op("vector", lambda e: e.tensor_tensor(
        Ps[:], dts_rep[:], acols[:], ALU.mult),
        reads=["acols"] + [("dtsrep", g) for g in range(16)], writes=["p2Ps"])
    p.op("scalar", lambda e: e.activation(Ps[:], Ps[:], AF.Exp),
        reads=["p2Ps"], writes=["p2Ps"])
    zero = c.sb("p2_zero", [128, 1], F32)
    p.op("gpsimd", lambda e: e.memset(zero[:], 0.0), writes=["p2zero"])
    abufs = [c.sb(f"p2_a{i}", [128, NT], F32) for i in range(2)]
    ys_scr = c.sb("p2_ys", [128, NT], BF16)
    for g in range(16):
        ak = build_a_group(c, g, dt_dram, abufs[g % 2], acols, g % 2)
        p.op("vector", lambda e, g=g: e.tensor_tensor_scan(
            ys_scr[:], abufs[g % 2][:], xp_sb[:, g * NT:(g + 1) * NT],
            zero[:], ALU.mult, ALU.add),
            reads=[ak, xp_keys[g], "p2zero"], writes=["p2ys"])
        p.op("vector", lambda e, g=g: e.tensor_copy(
            ends[:, g:g + 1], ys_scr[:, NT - 1:NT]),
            reads=["p2ys"], writes=["p2ends"])
    dma(p, "sync", sumP_ap, Ps[:], reads=["p2Ps"], writes=["summ_out"])
    dma(p, "sync", sumE_ap, ends[:], reads=["p2ends"], writes=["summ_out"])
    p.join_all()


def scan_passB(c, xp_sb, xp_keys, dt_dram, logA_dram, priorP_dram, priorE_dram,
               ys_sb):
    p = c.p
    acols = load_acols(c, logA_dram)
    carry = c.sb("p3_carry", [128, 16], F32)
    priP = c.sb("p3_priP", [128, 16], F32)
    priE = c.sb("p3_priE", [128, 16], F32)
    p.op("gpsimd", lambda e: e.memset(carry[:], 0.0), writes=["carry"])
    for j in range(3):
        dma(p, "sync", priP[:], priorP_dram.ap()[j * 128:(j + 1) * 128, :],
            reads=[], writes=["priP"])
        dma(p, "sync", priE[:], priorE_dram.ap()[j * 128:(j + 1) * 128, :],
            reads=[], writes=["priE"])
        p.op("vector", lambda e: e.tensor_tensor(
            carry[:], priP[:], carry[:], ALU.mult),
            reads=["priP", "carry"], writes=["carry"])
        p.op("vector", lambda e: e.tensor_tensor(
            carry[:], carry[:], priE[:], ALU.add),
            reads=["carry", "priE"], writes=["carry"])
    abufs = [c.sb(f"p3_a{i}", [128, NT], F32) for i in range(2)]
    yskeys = []
    for g in range(16):
        ak = build_a_group(c, g, dt_dram, abufs[g % 2], acols, g % 2)
        k = ("ys", g)
        p.op("vector", lambda e, g=g: e.tensor_tensor_scan(
            ys_sb[:, g * NT:(g + 1) * NT], abufs[g % 2][:],
            xp_sb[:, g * NT:(g + 1) * NT], carry[:, g:g + 1],
            ALU.mult, ALU.add),
            reads=[ak, xp_keys[g], "carry"], writes=[k])
        yskeys.append(k)
    return yskeys


def phase_ffn(c, xnT_dram, WguT_dram, WdownT_dram, g0T_dram, scratch_row):
    p = c.p
    nc = c.nc
    NGU = GUP // 128 // 2
    NKD = I_FFN // 128
    for tc in range(2):
        T = 1024
        t0g = tc * T
        with c.scope():
            h_sb = c.sb("f_h", [128, NKD * T], BF16)
            y_sb = c.sb("f_y", [128, 16 * T], BF16)
            hkeys = []
            inner = c.scope()
            inner.__enter__()
            xn_sb = c.sb("f_xn", [128, 16 * T], BF16)
            sg = c.sb("f_sg", [128, T], BF16)
            wg_bufs = [c.sb(f"f_wg{i}", [128, 16 * 2 * 128], BF16)
                       for i in range(2)]
            psg = c.ps("f_psg", [128, T], F32)
            psu = c.ps("f_psu", [128, T], F32)
            xk = []
            for g in range(16):
                k = ("fxn", tc, g)
                dma(p, "scalar", xn_sb[:, g * T:(g + 1) * T],
                    xnT_dram[g * 128:(g + 1) * 128, t0g:t0g + T],
                    reads=[], writes=[k])
                xk.append(k)
            for obp in range(NGU):
                sbuf = wg_bufs[obp % 2]
                skey = ("fwg", obp % 2)
                for half, ob in enumerate((obp, NGU + obp)):
                    src = WguT_dram[:, ob * 128:(ob + 1) * 128]
                    src = src.rearrange("(kb q) o -> q kb o", q=128)
                    dst = sbuf[:, half * 16 * 128:(half + 1) * 16 * 128]
                    dst = dst.rearrange("q (kb o) -> q kb o", kb=16)
                    dma(p, "sync", dst, src,
                        reads=[], writes=[skey] if half == 0 else [skey],
                        tag=f"fw{obp}")
                for half, ps in enumerate((psg, psu)):
                    for kb in range(16):
                        st = sbuf[:, (half * 16 + kb) * 128:
                                  (half * 16 + kb) * 128 + 128]
                        for q in range(2):
                            p.op("tensor",
                                 lambda e, st=st, ps=ps, kb=kb, q=q: e.matmul(
                                     ps[:, q * 512:(q + 1) * 512], st,
                                     xn_sb[:, kb * T + q * 512:kb * T + (q + 1) * 512],
                                     start=(kb == 0), stop=(kb == 15)),
                                 reads=[skey] + xk, writes=[("fps", half)])
                p.op("scalar", lambda e: e.activation(sg[:], psg[:], AF.Silu),
                     reads=[("fps", 0)], writes=["fsg"])
                hk = ("fh", tc, obp)
                p.op("vector", lambda e, obp=obp: e.tensor_tensor(
                    h_sb[:, obp * T:(obp + 1) * T], sg[:], psu[:], ALU.mult),
                    reads=["fsg", ("fps", 1)], writes=[hk])
                hkeys.append(hk)
            p.join_all()
            inner.__exit__(None, None, None)
            wd_bufs = [c.sb(f"f_wd{i}", [128, NKD * 128], BF16)
                       for i in range(2)]
            pst = [c.ps(f"f_pst{i}", [128, 512], F32) for i in range(2)]
            sums_ps = c.ps("f_sums", [1, T], F32)
            ones_col = c.sb("f_ones", [128, 1], BF16)
            ysq = c.sb("f_ysq", [128, T], BF16)
            p.op("gpsimd", lambda e: e.memset(ones_col[:], 1.0),
                 writes=["fones"])

            def act_view(kb, a, b):
                return h_sb[:, kb * T + a: kb * T + b]

            def evac(dg, a, b, ps, pskey):
                p.op("vector", lambda e: e.tensor_copy(
                    y_sb[:, dg * T + a: dg * T + b], ps[:, 0:b - a]),
                    reads=[pskey], writes=[("fy", dg)])
                p.op("scalar", lambda e: e.activation(
                    ysq[:, a:b], ps[:, 0:b - a], AF.Square),
                    reads=[pskey], writes=[("fysq", a)])
                p.op("tensor", lambda e: e.matmul(
                    sums_ps[0:1, a:b], ones_col[:], ysq[:, a:b],
                    start=(dg == 0), stop=(dg == 15)),
                    reads=[("fysq", a), "fones"], writes=["fsums"])

            mm_chain(c, f"fdown{tc}", WdownT_dram, I_FFN, D, T, act_view,
                     hkeys, evac, pst, wd_bufs, ob_group=1)
            row = c.sb("f_row", [1, T], F32)
            ceps, cepsk = c.constf(RMS_EPS)
            p.op("vector", lambda e: e.tensor_scalar(
                row[:], sums_ps[0:1, :], 1.0 / D, None, ALU.mult),
                reads=["fsums"], writes=["frow"])
            p.op("scalar", lambda e: e.activation(
                row[:], row[:], AF.Ln, bias=ceps[0:1, 0:1]),
                reads=["frow", cepsk], writes=["frow"])
            p.op("scalar", lambda e: e.activation(
                row[:], row[:], AF.Exp, scale=-0.5),
                reads=["frow"], writes=["frow"])
            dma(p, "gpsimd", scratch_row.ap()[0:1, 0:T], row[:],
                reads=["frow"], writes=["frowd"])
            rrep = c.sb("f_rrep", [128, T], F32)
            dma(p, "gpsimd", rrep[:],
                bass.AP(scratch_row, 0, [[0, 128], [1, T]]),
                reads=["frowd"], writes=["frrep"])
            g0b = c.sb("f_g0", [128, T], BF16)
            for dg in range(16):
                p.op("vector", lambda e, dg=dg: e.tensor_tensor(
                    g0b[:], y_sb[:, dg * T:(dg + 1) * T], rrep[:], ALU.mult),
                    reads=[("fy", dg), "frrep"], writes=["fg0b"])
                p.op("scalar", lambda e: e.activation(g0b[:], g0b[:], AF.Silu),
                     reads=["fg0b"], writes=["fg0b"])
                dma(p, "gpsimd", g0T_dram[dg * 128:(dg + 1) * 128, t0g:t0g + T],
                    g0b[:], reads=["fg0b"], writes=[("fg0out", tc, dg)])
            p.join_all()


def phase_gate_ln(c, xT_dram, sT_dram, gT_dram, x1T_dram, xn1T_dram,
                  scratch_row, xn1T32_dram=None):
    p = c.p
    for tc in range(2):
        T = 1024
        t0 = tc * T
        with c.scope():
            xt = c.sb("g_x", [128, T], F32)
            st = c.sb("g_s", [128, T], BF16)
            gt = c.sb("g_g", [128, T], BF16)
            sgf = c.sb("g_sg", [128, T], F32)
            x1b = c.sb("g_x1b", [128, T], BF16)
            sqb = c.sb("g_sq", [128, T], BF16)
            ones_col = c.sb("g_ones", [128, 1], BF16)
            sums_ps = c.ps("g_sums", [1, T], F32)
            sumsq_ps = c.ps("g_sumsq", [1, T], F32)
            p.op("gpsimd", lambda e: e.memset(ones_col[:], 1.0),
                 writes=["gones"])
            for dg in range(16):
                kx, ks, kg = "gx", "gs", "gg"
                dma(p, "scalar", xt[:], xT_dram[dg * 128:(dg + 1) * 128,
                                                t0:t0 + T],
                    reads=[], writes=[kx])
                dma(p, "sync", st[:], sT_dram[dg * 128:(dg + 1) * 128,
                                              t0:t0 + T],
                    reads=[], writes=[ks])
                dma(p, "gpsimd", gt[:], gT_dram[dg * 128:(dg + 1) * 128,
                                                t0:t0 + T],
                    reads=[], writes=[kg])
                p.op("vector", lambda e: e.tensor_tensor(
                    sgf[:], st[:], gt[:], ALU.mult),
                    reads=[ks, kg], writes=["gsg"])
                p.op("vector", lambda e: e.tensor_tensor(
                    sgf[:], xt[:], sgf[:], ALU.add),
                    reads=[kx, "gsg"], writes=["gx1"])
                dma(p, "sync", x1T_dram[dg * 128:(dg + 1) * 128, t0:t0 + T],
                    sgf[:], reads=["gx1"], writes=[("gx1out", tc, dg)])
                p.op("vector", lambda e: e.tensor_copy(x1b[:], sgf[:]),
                     reads=["gx1"], writes=["gx1b"])
                p.op("scalar", lambda e: e.activation(
                    sqb[:], sgf[:], AF.Square),
                    reads=["gx1"], writes=["gsq"])
                p.op("tensor", lambda e, dg=dg: e.matmul(
                    sums_ps[0:1, 0:512], ones_col[:], x1b[:, 0:512],
                    start=(dg == 0), stop=(dg == 15)),
                    reads=["gx1b", "gones"], writes=["gsums"])
                p.op("tensor", lambda e, dg=dg: e.matmul(
                    sums_ps[0:1, 512:1024], ones_col[:], x1b[:, 512:1024],
                    start=(dg == 0), stop=(dg == 15)),
                    reads=["gx1b", "gones"], writes=["gsums"])
                p.op("tensor", lambda e, dg=dg: e.matmul(
                    sumsq_ps[0:1, 0:512], ones_col[:], sqb[:, 0:512],
                    start=(dg == 0), stop=(dg == 15)),
                    reads=["gsq", "gones"], writes=["gsumsq"])
                p.op("tensor", lambda e, dg=dg: e.matmul(
                    sumsq_ps[0:1, 512:1024], ones_col[:], sqb[:, 512:1024],
                    start=(dg == 0), stop=(dg == 15)),
                    reads=["gsq", "gones"], writes=["gsumsq"])
            rowm = c.sb("g_rm", [1, T], F32)
            rowv = c.sb("g_rv", [1, T], F32)
            rowt = c.sb("g_rt", [1, T], F32)
            ceps, cepsk = c.constf(LN_EPS)
            p.op("vector", lambda e: e.tensor_scalar(
                rowm[:], sums_ps[0:1, :], 1.0 / D, None, ALU.mult),
                reads=["gsums"], writes=["grm"])
            p.op("vector", lambda e: e.tensor_scalar(
                rowv[:], sumsq_ps[0:1, :], 1.0 / D, None, ALU.mult),
                reads=["gsumsq"], writes=["grv"])
            p.op("vector", lambda e: e.tensor_tensor(
                rowt[:], rowm[:], rowm[:], ALU.mult),
                reads=["grm"], writes=["grt"])
            p.op("vector", lambda e: e.tensor_tensor(
                rowv[:], rowv[:], rowt[:], ALU.subtract),
                reads=["grv", "grt"], writes=["grv"])
            p.op("scalar", lambda e: e.activation(
                rowv[:], rowv[:], AF.Ln, bias=ceps[0:1, 0:1]),
                reads=["grv", cepsk], writes=["grv"])
            p.op("scalar", lambda e: e.activation(
                rowv[:], rowv[:], AF.Exp, scale=-0.5),
                reads=["grv"], writes=["grv"])
            p.op("vector", lambda e: e.tensor_tensor(
                rowm[:], rowm[:], rowv[:], ALU.mult),
                reads=["grm", "grv"], writes=["grm"])
            dma(p, "gpsimd", scratch_row.ap()[0:1, 0:T], rowv[:],
                reads=["grv"], writes=["grvd"])
            dma(p, "gpsimd", scratch_row.ap()[1:2, 0:T], rowm[:],
                reads=["grm"], writes=["grmd"])
            rrep = c.sb("g_rrep", [128, T], F32)
            mrep = c.sb("g_mrep", [128, T], F32)
            dma(p, "gpsimd", rrep[:], bass.AP(scratch_row, 0, [[0, 128], [1, T]]),
                reads=["grvd"], writes=["grrep"])
            dma(p, "gpsimd", mrep[:],
                bass.AP(scratch_row, 2048, [[0, 128], [1, T]]),
                reads=["grmd"], writes=["gmrep"])
            xn1b = c.sb("g_xn1", [128, T], BF16)
            for dg in range(16):
                kx = "gx"
                dma(p, "scalar", xt[:], x1T_dram[dg * 128:(dg + 1) * 128,
                                                 t0:t0 + T],
                    reads=[("gx1out", tc, dg)], writes=[kx])
                p.op("vector", lambda e: e.tensor_tensor(
                    sgf[:], xt[:], rrep[:], ALU.mult),
                    reads=[kx, "grrep"], writes=["gxn1f"])
                p.op("vector", lambda e: e.tensor_tensor(
                    sgf[:], sgf[:], mrep[:], ALU.subtract),
                    reads=["gxn1f", "gmrep"], writes=["gxn1f"])
                p.op("vector", lambda e: e.tensor_copy(xn1b[:], sgf[:]),
                     reads=["gxn1f"], writes=["gxn1b"])
                dma(p, "sync", xn1T_dram[dg * 128:(dg + 1) * 128, t0:t0 + T],
                    xn1b[:], reads=["gxn1b"], writes=[("gxn1out", tc, dg)])
                if xn1T32_dram is not None:
                    dma(p, "gpsimd",
                        xn1T32_dram[dg * 128:(dg + 1) * 128, t0:t0 + T],
                        sgf[:], reads=["gxn1f"], writes=[("gxn32out", tc, dg)])
            p.join_all()


def emit_cast(c, src_dram, dst_dram, rows, cols, name):
    p = c.p
    bufs = [c.sb(f"{name}_cb{i}", [128, cols], BF16) for i in range(2)]
    for ti in range(rows // 128):
        buf = bufs[ti % 2]
        k = (name, "cb", ti % 2)
        dma(p, "gpsimd", buf[:], src_dram[ti * 128:(ti + 1) * 128, :],
            reads=[], writes=[k])
        dma(p, "sync", dst_dram[ti * 128:(ti + 1) * 128, :], buf[:],
            reads=[k], writes=[(name, "out", ti)])


def emit_dequant(c, pk_dram, scE_dram, n_out, W, out_dram, name):
    p = c.p
    LN2 = float(np.log(2.0))
    WC = 32
    M = 8 * WC
    pk = c.sb(f"{name}_pk", [128, W], I32)
    sh = c.sb(f"{name}_sh", [128, WC], I32)
    nb = c.sb(f"{name}_nb", [128, M], I32)
    mi = c.sb(f"{name}_mi", [128, M], I32)
    nf = c.sb(f"{name}_nf", [128, M], F32)
    m8 = c.sb(f"{name}_m8", [128, M], F32)
    em = c.sb(f"{name}_em", [128, M], F32)
    mm = c.sb(f"{name}_mm", [128, M], F32)
    e2 = c.sb(f"{name}_e2", [128, M], F32)
    cf = c.sb(f"{name}_cf", [128, M], F32)
    val = c.sb(f"{name}_val", [128, M], F32)
    sc = c.sb(f"{name}_sc", [128, M], F32)
    ob = c.sb(f"{name}_ob", [128, M], BF16)

    def v(fn, rd, wr):
        p.op("vector", fn, reads=rd, writes=wr)

    for ti in range(n_out // 128):
        r0 = ti * 128
        dma(p, "sync", pk[:], pk_dram[r0:r0 + 128, :], reads=[], writes=["dpk"])
        for w0 in range(0, W, WC):
            w1 = min(w0 + WC, W)
            wn = w1 - w0
            mn = 8 * wn
            dma(p, "gpsimd", sc[:, 0:mn],
                scE_dram[r0:r0 + 128, 8 * w0:8 * w1], reads=[], writes=["dsc"])
            for k in range(8):
                v(lambda e, k=k, w0=w0, wn=wn: e.tensor_single_scalar(
                    sh[:, 0:wn], pk[:, w0:w0 + wn], 4 * k,
                    ALU.logical_shift_right), ["dpk"], ["dsh"])
                v(lambda e, k=k, wn=wn: e.tensor_single_scalar(
                    bass.AP(nb, k, [[M, 128], [8, wn]]), sh[:, 0:wn], 15,
                    ALU.bitwise_and), ["dsh"], ["dnb"])
            v(lambda e, mn=mn: e.tensor_single_scalar(
                mi[:, 0:mn], nb[:, 0:mn], 1, ALU.bitwise_and),
              ["dnb"], ["dmi"])
            v(lambda e, mn=mn: e.tensor_copy(mm[:, 0:mn], mi[:, 0:mn]),
              ["dmi"], ["dmm"])
            v(lambda e, mn=mn: e.tensor_single_scalar(
                mi[:, 0:mn], nb[:, 0:mn], 3, ALU.logical_shift_right),
              ["dnb", "dmi"], ["dmi"])
            v(lambda e, mn=mn: e.tensor_copy(m8[:, 0:mn], mi[:, 0:mn]),
              ["dmi"], ["dm8"])
            v(lambda e, mn=mn: e.tensor_copy(nf[:, 0:mn], nb[:, 0:mn]),
              ["dnb"], ["dnf"])
            v(lambda e, mn=mn: e.scalar_tensor_tensor(
                em[:, 0:mn], m8[:, 0:mn], -8.0, nf[:, 0:mn],
                ALU.mult, ALU.add), ["dm8", "dnf"], ["dem"])
            v(lambda e, mn=mn: e.tensor_tensor(
                e2[:, 0:mn], em[:, 0:mn], mm[:, 0:mn], ALU.subtract),
              ["dem", "dmm"], ["de2"])
            p.op("scalar", lambda e, mn=mn: e.activation(
                e2[:, 0:mn], e2[:, 0:mn], AF.Exp, scale=LN2 / 2),
                 reads=["de2"], writes=["de2"])
            v(lambda e, mn=mn: e.tensor_scalar(
                cf[:, 0:mn], mm[:, 0:mn], 0.25, 0.5, ALU.mult, ALU.add),
              ["dmm"], ["dcf"])
            v(lambda e, mn=mn: e.tensor_tensor(
                cf[:, 0:mn], cf[:, 0:mn], e2[:, 0:mn], ALU.mult),
              ["dcf", "de2"], ["dcf"])
            v(lambda e, mn=mn: e.tensor_single_scalar(
                val[:, 0:mn], em[:, 0:mn], 2.0, ALU.is_lt), ["dem"], ["dval"])
            v(lambda e, mn=mn: e.tensor_scalar(
                em[:, 0:mn], em[:, 0:mn], 0.5, None, ALU.mult),
              ["dem"], ["dem"])
            v(lambda e, mn=mn: e.tensor_tensor(
                em[:, 0:mn], em[:, 0:mn], cf[:, 0:mn], ALU.subtract),
              ["dem", "dcf"], ["dem"])
            v(lambda e, mn=mn: e.tensor_tensor(
                val[:, 0:mn], val[:, 0:mn], em[:, 0:mn], ALU.mult),
              ["dval", "dem"], ["dval"])
            v(lambda e, mn=mn: e.tensor_tensor(
                val[:, 0:mn], val[:, 0:mn], cf[:, 0:mn], ALU.add),
              ["dval", "dcf"], ["dval"])
            v(lambda e, mn=mn: e.tensor_scalar(
                m8[:, 0:mn], m8[:, 0:mn], -2.0, 1.0, ALU.mult, ALU.add),
              ["dm8"], ["dm8"])
            v(lambda e, mn=mn: e.tensor_tensor(
                val[:, 0:mn], val[:, 0:mn], m8[:, 0:mn], ALU.mult),
              ["dval", "dm8"], ["dval"])
            v(lambda e, mn=mn: e.tensor_tensor(
                ob[:, 0:mn], val[:, 0:mn], sc[:, 0:mn], ALU.mult),
              ["dval", "dsc"], ["dob"])
            dma(p, "scalar", out_dram[r0:r0 + 128, 8 * w0:8 * w1],
                ob[:, 0:mn], reads=["dob"], writes=[(name, "dq", ti)])


def build_launch_PREP():
    nc = bass.Bass("TRN2", target_bir_lowering=False)
    KS = D // 8
    ins = {}
    outs = {}
    specs = [("WinT", KS, D), ("WoutT", KS, D), ("WoT", KS, D),
             ("WqkvT", KS, 3 * D), ("WdtT", KS, 32),
             ("W1s", NE * D // 8, EHID), ("W2s", NE * EHID // 8, D)]
    for nm, r, cl in specs:
        ins[nm] = nc.dram_tensor(nm + "32", [r, cl], F32, kind="ExternalInput")
        outs[nm] = nc.dram_tensor(nm + "bf", [r, cl], BF16,
                                  kind="ExternalOutput")
    gup_pk = nc.dram_tensor("gup_pk", [GUP, 32], I32, kind="ExternalInput")
    gup_sc = nc.dram_tensor("gup_sc", [GUP, 256], F16, kind="ExternalInput")
    down_pk = nc.dram_tensor("down_pk", [D, 88], I32, kind="ExternalInput")
    down_sc = nc.dram_tensor("down_sc", [D, 704], F16, kind="ExternalInput")
    gup_dq = nc.dram_tensor("gup_dq", [GUP, 256], BF16, kind="ExternalOutput")
    down_dq = nc.dram_tensor("down_dq", [D, 704], BF16, kind="ExternalOutput")
    stack = contextlib.ExitStack()
    with stack:
        c = Ctx(nc, stack)
        with c.scope():
            for nm, r, cl in specs:
                emit_cast(c, ins[nm].ap(), outs[nm].ap(), r, cl, nm)
            c.p.join_all()
        with c.scope():
            emit_dequant(c, gup_pk.ap(), gup_sc.ap(), GUP, 32, gup_dq.ap(),
                         "dqg")
            emit_dequant(c, down_pk.ap(), down_sc.ap(), D, 88, down_dq.ap(),
                         "dqd")
            c.p.join_all()
        c.p.finish()
    return nc


def phase_qkv(c, xn1T_dram, WqkvT_dram, cos_dram, sinN_dram,
              qr_dram, kr_dram, v_dram, wdep=()):
    p = c.p
    with c.scope():
        act_sb = c.sb("q_act", [128, 16 * NT], BF16)
        act_keys = load_actT(c, "qact", xn1T_dram, act_sb, 0)
        cos_sb = c.sb("q_cos", [128, NT], F32)
        sin_sb = c.sb("q_sin", [128, NT], F32)
        dma(p, "sync", cos_sb[:], cos_dram, reads=[], writes=["qcos"])
        dma(p, "sync", sin_sb[:], sinN_dram, reads=[], writes=["qsin"])
        hb = [c.sb(f"q_hb{i}", [128, NT], BF16) for i in range(2)]
        t1 = c.sb("q_t1", [128, NT], F32)
        t2 = c.sb("q_t2", [128, NT], F32)
        wbufs = [c.sb(f"q_wt{i}", [128, 16 * 2 * 128], BF16) for i in range(2)]
        pst = [c.ps(f"q_ps{i}", [128, 512], F32) for i in range(4)]

        def act_view(kb, a, b):
            return act_sb[:, kb * NT + a: kb * NT + b]

        def evac(ob, a, b, ps, pskey):
            buf = hb[ob % 2]
            bk = ("qhb", ob % 2)
            p.op("vector", lambda e: e.tensor_copy(buf[:, a:b], ps[:, 0:b - a]),
                 reads=[pskey], writes=[bk])
            if b < NT:
                return
            if ob < 32:
                dest = qr_dram if ob < 16 else kr_dram
                h = ob if ob < 16 else ob - 16
                p.op("vector", lambda e: e.tensor_tensor(
                    t1[:], buf[:], cos_sb[:], ALU.mult),
                    reads=[bk, "qcos"], writes=["qt1"])
                p.op("vector", lambda e: e.tensor_copy(
                    t2[0:64, :], buf[64:128, :]),
                    reads=[bk], writes=["qt2"])
                p.op("vector", lambda e: e.tensor_copy(
                    t2[64:128, :], buf[0:64, :]),
                    reads=[bk], writes=["qt2"])
                p.op("vector", lambda e: e.tensor_tensor(
                    t2[:], t2[:], sin_sb[:], ALU.mult),
                    reads=["qt2", "qsin"], writes=["qt2"])
                p.op("vector", lambda e: e.tensor_tensor(
                    buf[:], t1[:], t2[:], ALU.add),
                    reads=["qt1", "qt2"], writes=[bk])
                dma(p, "gpsimd", dest[h * 128:(h + 1) * 128, :], buf[:],
                    reads=[bk], writes=[("qout", ob)])
            else:
                h = ob - 32
                dma(p, "gpsimd", v_dram[h * 128:(h + 1) * 128, :], buf[:],
                    reads=[bk], writes=[("qout", ob)])

        mm_chain(c, "qkv", WqkvT_dram, D, 3 * D, NT, act_view, act_keys, evac,
                 pst, wbufs, wt_deps=wdep)
        p.join_all()


def phase_attn(c, qa_dram, ka_dram, va_dram, ya_dram, row_scr, idb):
    p = c.p
    EXPC = 20.0
    for b in range(2):
        for hsel in range(2):
            with c.scope():
                Ksb = c.sb("a_K", [128, 4 * NT], BF16)
                Qsb = c.sb("a_Q", [128, 4 * NT], BF16)
                Vsb = c.sb("a_V", [128, 64 * 128], BF16)
                kq_keys = []
                for i in range(4):
                    src = 4 * b + i
                    off = (src * 256 + hsel * 128) * NT
                    kk = ("aK", i)
                    dma(p, "sync", Ksb[:, i * NT:(i + 1) * NT],
                        bass.AP(ka_dram, off, [[NT, 128], [1, NT]]),
                        reads=[], writes=[kk])
                    qk = ("aQ", i)
                    dma(p, "scalar", Qsb[:, i * NT:(i + 1) * NT],
                        bass.AP(qa_dram, off, [[NT, 128], [1, NT]]),
                        reads=[], writes=[qk])
                    kq_keys += [kk, qk]
                    for pb in range(16):
                        vk = ("aV", i, pb)
                        p.op("sync", lambda e, off=off, i=i, pb=pb:
                             e.dma_start_transpose(
                                 Vsb[:, (i * 16 + pb) * 128:
                                     (i * 16 + pb) * 128 + 128],
                                 bass.AP(va_dram, off + pb * 128,
                                         [[NT, 128], [1, 128]])),
                             reads=[], writes=[vk], dma=True)
                        kq_keys.append(vk)
                sps = [c.ps(f"a_s{i}", [128, 512], F32) for i in range(2)]
                yps = c.ps("a_y", [128, 512], F32)
                dps = c.ps("a_d", [1, 512], F32)
                phat = c.sb("a_ph", [128, 8 * 512], BF16)
                ones_col = c.sb("a_ones", [128, 1], BF16)
                p.op("gpsimd", lambda e: e.memset(ones_col[:], 1.0),
                     writes=["aones"])
                cm20, cm20k = c.constf(-EXPC)
                dmask = c.sb("a_dm", [128, 4 * 512], BF16)
                dmi = c.sb("a_dmi", [128, 4 * 512], F32)
                p.op("gpsimd", lambda e: e.iota(
                    dmi[:].rearrange("p (d q) -> p d q", d=4),
                    [[-128, 4], [1, 512]], base=0, channel_multiplier=-1,
                    allow_small_or_imprecise_dtypes=True),
                    writes=["admi"])
                p.op("vector", lambda e: e.tensor_single_scalar(
                    dmask[:], dmi[:], 0.0, ALU.is_ge),
                    reads=["admi"], writes=["adm"])
                ydiv = c.sb("a_yd", [128, 512], BF16)
                rrep = c.sb("a_rr", [128, 512], F32)
                drow = c.sb("a_dr", [1, 512], F32)
                ystg = c.sb("a_ys", [128, 4 * 128], BF16)
                yt_ps = c.ps("a_yt", [128, 128], F32)
                for qc in range(16):
                    nkb = (qc + 1) * 4
                    q0 = qc * 512
                    for kb in range(nkb):
                        ps = sps[kb % 2]
                        psk = ("asps", kb % 2)
                        p.op("tensor", lambda e, kb=kb, q0=q0, ps=ps: e.matmul(
                            ps[:], Ksb[:, kb * 128:(kb + 1) * 128],
                            Qsb[:, q0:q0 + 512], start=True, stop=True),
                            reads=kq_keys, writes=[psk])
                        pk = ("aphat", kb % 8)
                        p.op("scalar", lambda e, kb=kb, ps=ps: e.activation(
                            phat[:, (kb % 8) * 512:(kb % 8 + 1) * 512], ps[:],
                            AF.Exp, bias=cm20[:, 0:1], scale=float(ADH) ** -0.5),
                            reads=[psk, cm20k], writes=[pk])
                        if kb >= qc * 4:
                            d = kb - qc * 4
                            p.op("vector", lambda e, kb=kb, d=d: e.tensor_tensor(
                                phat[:, (kb % 8) * 512:(kb % 8 + 1) * 512],
                                phat[:, (kb % 8) * 512:(kb % 8 + 1) * 512],
                                dmask[:, d * 512:(d + 1) * 512], ALU.mult),
                                reads=[pk, "adm"], writes=[pk])
                        p.op("tensor", lambda e, kb=kb: e.matmul(
                            dps[0:1, :], ones_col[:],
                            phat[:, (kb % 8) * 512:(kb % 8 + 1) * 512],
                            start=(kb == 0), stop=(kb == nkb - 1)),
                            reads=[pk, "aones"], writes=["adps"])
                        p.op("tensor", lambda e, kb=kb: e.matmul(
                            yps[:], Vsb[:, kb * 128:(kb + 1) * 128],
                            phat[:, (kb % 8) * 512:(kb % 8 + 1) * 512],
                            start=(kb == 0), stop=(kb == nkb - 1)),
                            reads=[pk] + kq_keys, writes=["ayps"])
                    p.op("vector", lambda e: e.tensor_copy(drow[:], dps[0:1, :]),
                         reads=["adps"], writes=["adrow"])
                    p.op("vector", lambda e: e.reciprocal(drow[:], drow[:]),
                         reads=["adrow"], writes=["adrow"])
                    dma(p, "gpsimd", row_scr.ap()[0:1, 0:512], drow[:],
                        reads=["adrow"], writes=["adrd"])
                    dma(p, "gpsimd", rrep[:],
                        bass.AP(row_scr, 0, [[0, 128], [1, 512]]),
                        reads=["adrd"], writes=["arrep"])
                    p.op("vector", lambda e: e.tensor_tensor(
                        ydiv[:], yps[:], rrep[:], ALU.mult),
                        reads=["ayps", "arrep"], writes=["aydiv"])
                    for blk in range(4):
                        p.op("tensor", lambda e, blk=blk: e.matmul(
                            yt_ps[:], ydiv[:, blk * 128:(blk + 1) * 128],
                            idb[:], start=True, stop=True),
                            reads=["aydiv", "identb"], writes=["aytps"])
                        p.op("vector", lambda e, blk=blk: e.tensor_copy(
                            ystg[:, blk * 128:(blk + 1) * 128], yt_ps[:]),
                            reads=["aytps"], writes=["aystg"])
                    dst = ya_dram[b * 8192 + q0: b * 8192 + q0 + 512,
                                  hsel * 128:(hsel + 1) * 128]
                    dst = dst.rearrange("(blk p) c -> p blk c", p=128)
                    dma(p, "sync", dst,
                        ystg[:].rearrange("p (blk c) -> p blk c", blk=4),
                        reads=["aystg"], writes=[("ayout", b, hsel, qc)])
                p.join_all()


def phase_wo(c, yb_dram, WoT_dram, s1T_dram):
    p = c.p
    with c.scope():
        act_sb = c.sb("wo_act", [128, 16 * NT], BF16)
        act_keys = []
        for g in range(16):
            j, cb = g // 2, g % 2
            k = ("woact", g)
            for t4 in range(4):
                p.op("sync", lambda e, g=g, j=j, cb=cb, t4=t4:
                     e.dma_start_transpose(
                         act_sb[:, g * NT + t4 * 512: g * NT + (t4 + 1) * 512],
                         bass.AP(yb_dram, (j * 2048 + t4 * 512) * 256 + cb * 128,
                                 [[256, 512], [1, 128]])),
                     reads=[], writes=[k], dma=True)
            act_keys.append(k)
        wbufs = [c.sb(f"wo_wt{i}", [128, 16 * 2 * 128], BF16) for i in range(2)]
        pst = [c.ps(f"wo_ps{i}", [128, 512], F32) for i in range(4)]
        evac = make_evac_dram(c, "s1", s1T_dram)

        def act_view(kb, a, bnd):
            return act_sb[:, kb * NT + a: kb * NT + bnd]

        mm_chain(c, "womm", WoT_dram, D, D, NT, act_view, act_keys, evac,
                 pst, wbufs)
        p.join_all()


def phase_final(c, x1T_dram, s1T_dram, g1T_dram, out_dram, idf):
    p = c.p
    with c.scope():
        xt = c.sb("o_x", [128, NT], F32)
        st = c.sb("o_s", [128, NT], BF16)
        gt = c.sb("o_g", [128, NT], BF16)
        x2 = c.sb("o_x2", [128, NT], F32)
        stg = c.sb("o_stg", [128, 16 * 128], F32)
        tps = [c.ps(f"o_tps{i}", [128, 128], F32) for i in range(2)]
        for dg in range(16):
            dma(p, "scalar", xt[:], x1T_dram[dg * 128:(dg + 1) * 128, :],
                reads=[], writes=["ox"])
            dma(p, "sync", st[:], s1T_dram[dg * 128:(dg + 1) * 128, :],
                reads=[], writes=["os"])
            dma(p, "gpsimd", gt[:], g1T_dram[dg * 128:(dg + 1) * 128, :],
                reads=[], writes=["og"])
            p.op("vector", lambda e: e.tensor_tensor(
                x2[:], st[:], gt[:], ALU.mult),
                reads=["os", "og"], writes=["ox2"])
            p.op("vector", lambda e: e.tensor_tensor(
                x2[:], xt[:], x2[:], ALU.add),
                reads=["ox", "ox2"], writes=["ox2"])
            for ti in range(16):
                ps = tps[ti % 2]
                pk = ("ops", ti % 2)
                p.op("tensor", lambda e, ti=ti, ps=ps: e.matmul(
                    ps[:], x2[:, ti * 128:(ti + 1) * 128], idf[:],
                    start=True, stop=True),
                    reads=["ox2", "identf"], writes=[pk])
                p.op("vector", lambda e, ti=ti, ps=ps: e.tensor_copy(
                    stg[:, ti * 128:(ti + 1) * 128], ps[:]),
                    reads=[pk], writes=["ostg"])
            dst = out_dram[:, dg * 128:(dg + 1) * 128]
            dst = dst.rearrange("(blk p) c -> p blk c", p=128)
            dma(p, "sync", dst, stg[:].rearrange("p (blk c) -> p blk c", blk=16),
                reads=["ostg"], writes=[("oout", dg)])
        p.join_all()


def phase_moe_gate(c, xn1T_dram, WgT_dram, bbias_dram, comb_dram, idf):
    p = c.p
    with c.scope():
        wg_sb = c.sb("m_wg", [128, 16 * 16], F32)
        dma(p, "sync", wg_sb[:].rearrange("q (kb o) -> q kb o", kb=16),
            WgT_dram.rearrange("(kb q) o -> q kb o", q=128),
            reads=[], writes=["mwg"])
        bb = c.sb("m_bb", [16, 1], F32)
        dma(p, "sync", bb[:], bbias_dram, reads=[], writes=["mbb"])
        xn_sb = c.sb("m_xng", [128, 16 * 512], F32)
        scores = c.sb("m_sc", [16, NT], F32)
        lps = c.ps("m_lps", [16, 512], F32)
        for tc in range(4):
            t0 = tc * 512
            xk = []
            for g in range(16):
                k = ("mxng", g)
                dma(p, "scalar", xn_sb[:, g * 512:(g + 1) * 512],
                    xn1T_dram[g * 128:(g + 1) * 128, t0:t0 + 512],
                    reads=[], writes=[k])
                xk.append(k)
            for kb in range(16):
                p.op("tensor", lambda e, kb=kb: e.matmul(
                    lps[:, 0:512], wg_sb[:, kb * 16:(kb + 1) * 16],
                    xn_sb[:, kb * 512:(kb + 1) * 512],
                    start=(kb == 0), stop=(kb == 15)),
                    reads=["mwg"] + xk, writes=["mlps"])
            p.op("vector", lambda e, t0=t0: e.tensor_scalar(
                scores[:, t0:t0 + 512], lps[:, 0:512], bb[:], None, ALU.add),
                reads=["mlps", "mbb"], writes=["msc"])
            p.op("scalar", lambda e, t0=t0: e.activation(
                scores[:, t0:t0 + 512], scores[:, t0:t0 + 512], AF.Sigmoid),
                reads=["msc"], writes=["msc"])
        iota16 = c.sb("m_io", [128, 16], F32)
        iota16p = c.sb("m_iop", [128, 16], F32)
        p.op("gpsimd", lambda e: e.iota(iota16[:], [[1, 16]], base=0,
                                        channel_multiplier=0,
                                        allow_small_or_imprecise_dtypes=True),
             writes=["mio"])
        p.op("vector", lambda e: e.tensor_scalar(
            iota16p[:], iota16[:], 16.0, None, ALU.add),
            reads=["mio"], writes=["miop"])
        tps = c.ps("m_tps", [128, 16], F32)
        bps = c.ps("m_bps", [16, 128], F32)
        tok = c.sb("m_tok", [128, 16], F32)
        eq = c.sb("m_eq", [128, 16], F32)
        s2 = c.sb("m_s2", [128, 16], F32)
        cols = c.sb("m_cols", [128, 6], F32)
        cmb = c.sb("m_cmb", [128, 16], F32)
        combT = c.sb("m_combT", [16, NT], F32)
        for ti in range(16):
            p.op("tensor", lambda e, ti=ti: e.matmul(
                tps[:], scores[:, ti * 128:(ti + 1) * 128], idf[0:16, 0:16],
                start=True, stop=True),
                reads=["msc", "identf"], writes=["mtps"])
            p.op("vector", lambda e: e.tensor_copy(tok[:], tps[:]),
                 reads=["mtps"], writes=["mtok"])
            p.op("vector", lambda e: e.tensor_reduce(
                cols[:, 0:1], tok[:], axis=mybir.AxisListType.X, op=ALU.max),
                reads=["mtok"], writes=["mcols"])
            p.op("vector", lambda e: e.tensor_scalar(
                eq[:], tok[:], cols[:, 0:1], None, ALU.is_equal),
                reads=["mtok", "mcols"], writes=["meq"])
            p.op("vector", lambda e: e.scalar_tensor_tensor(
                eq[:], eq[:], -16.0, iota16p[:], ALU.mult, ALU.add),
                reads=["meq", "miop"], writes=["meq"])
            p.op("vector", lambda e: e.tensor_reduce(
                cols[:, 1:2], eq[:], axis=mybir.AxisListType.X, op=ALU.min),
                reads=["meq"], writes=["mcols"])
            p.op("vector", lambda e: e.tensor_scalar(
                eq[:], iota16[:], cols[:, 1:2], None, ALU.is_equal),
                reads=["mio", "mcols"], writes=["meq"])
            p.op("vector", lambda e: e.scalar_tensor_tensor(
                s2[:], eq[:], -1e9, tok[:], ALU.mult, ALU.add),
                reads=["meq", "mtok"], writes=["ms2"])
            p.op("vector", lambda e: e.tensor_reduce(
                cols[:, 2:3], s2[:], axis=mybir.AxisListType.X, op=ALU.max),
                reads=["ms2"], writes=["mcols"])
            p.op("vector", lambda e: e.tensor_scalar(
                s2[:], s2[:], cols[:, 2:3], None, ALU.is_equal),
                reads=["ms2", "mcols"], writes=["ms2"])
            p.op("vector", lambda e: e.scalar_tensor_tensor(
                s2[:], s2[:], -16.0, iota16p[:], ALU.mult, ALU.add),
                reads=["ms2", "miop"], writes=["ms2"])
            p.op("vector", lambda e: e.tensor_reduce(
                cols[:, 3:4], s2[:], axis=mybir.AxisListType.X, op=ALU.min),
                reads=["ms2"], writes=["mcols"])
            p.op("vector", lambda e: e.tensor_scalar(
                s2[:], iota16[:], cols[:, 3:4], None, ALU.is_equal),
                reads=["mio", "mcols"], writes=["ms2"])
            p.op("vector", lambda e: e.tensor_tensor(
                cols[:, 4:5], cols[:, 0:1], cols[:, 2:3], ALU.add),
                reads=["mcols"], writes=["mcols"])
            p.op("vector", lambda e: e.tensor_scalar(
                cols[:, 4:5], cols[:, 4:5], TOPK_EPS, None, ALU.add),
                reads=["mcols"], writes=["mcols"])
            p.op("vector", lambda e: e.reciprocal(cols[:, 5:6], cols[:, 4:5]),
                 reads=["mcols"], writes=["mcols"])
            p.op("vector", lambda e: e.tensor_tensor(
                cmb[:], eq[:], s2[:], ALU.add),
                reads=["meq", "ms2"], writes=["mcmb"])
            p.op("vector", lambda e: e.tensor_tensor(
                cmb[:], cmb[:], tok[:], ALU.mult),
                reads=["mcmb", "mtok"], writes=["mcmb"])
            p.op("vector", lambda e: e.tensor_scalar(
                cmb[:], cmb[:], cols[:, 5:6], None, ALU.mult),
                reads=["mcmb", "mcols"], writes=["mcmb"])
            p.op("tensor", lambda e, ti=ti: e.matmul(
                bps[0:16, :], cmb[:], idf[:],
                start=True, stop=True),
                reads=["mcmb", "identf"], writes=["mbps"])
            p.op("vector", lambda e, ti=ti: e.tensor_copy(
                combT[:, ti * 128:(ti + 1) * 128], bps[0:16, :]),
                reads=["mbps"], writes=["mcombT"])
        dma(p, "sync", comb_dram.ap(), combT[:], reads=["mcombT"],
            writes=["mcombd"])
        p.join_all()


def phase_moe_experts(c, xn1T_dram, W1_dram, W2_dram, comb_dram, g1T_dram,
                      w1dep=(), w2dep=()):
    p = c.p
    T = 512
    for tc in range(4):
        t0 = tc * T
        with c.scope():
            xn_sb = c.sb("me_xn", [128, 16 * T], BF16)
            acc = c.sb("me_acc", [128, 16 * T], F32)
            z_sb = c.sb("me_z", [128, 8 * T], BF16)
            crep = c.sb("me_crep", [128, T], F32)
            sg = c.sb("me_sg", [128, T], BF16)
            w1b = [c.sb(f"me_w1{i}", [128, 16 * 8 * 128], BF16)
                   for i in range(2)]
            w2b = [c.sb(f"me_w2{i}", [128, 8 * 16 * 128], BF16)
                   for i in range(2)]
            hps = [c.ps(f"me_hps{i}", [128, T], F32) for i in range(2)]
            ops_ = [c.ps(f"me_ops{i}", [128, T], F32) for i in range(2)]
            xk = []
            for g in range(16):
                k = ("mexn", tc, g)
                dma(p, "scalar", xn_sb[:, g * T:(g + 1) * T],
                    xn1T_dram[g * 128:(g + 1) * 128, t0:t0 + T],
                    reads=[], writes=[k])
                xk.append(k)
            for e in range(NE):
                w1k = ("mew1", e % 2)
                src = W1_dram[e * 2048:(e + 1) * 2048, :]
                src = src.rearrange("(kb q) o -> q kb o", q=128)
                dma(p, "sync", w1b[e % 2][:].rearrange(
                    "q (kb o) -> q kb o", kb=16), src,
                    reads=list(w1dep), writes=[w1k])
                w2k = ("mew2", e % 2)
                src2 = W2_dram[e * 1024:(e + 1) * 1024, :]
                src2 = src2.rearrange("(kb q) o -> q kb o", q=128)
                dma(p, "gpsimd", w2b[e % 2][:].rearrange(
                    "q (kb o) -> q kb o", kb=8), src2,
                    reads=list(w2dep), writes=[w2k])
                ck = "mecrep"
                dma(p, "gpsimd", crep[:],
                    bass.AP(comb_dram, e * NT + t0, [[0, 128], [1, T]]),
                    reads=[], writes=[ck])
                for hb in range(8):
                    ps = hps[hb % 2]
                    for kb in range(16):
                        p.op("tensor", lambda e2, e=e, hb=hb, kb=kb, ps=ps:
                             e2.matmul(
                                 ps[:, 0:T],
                                 w1b[e % 2][:, (kb * 8 + hb) * 128:
                                            (kb * 8 + hb) * 128 + 128],
                                 xn_sb[:, kb * T:(kb + 1) * T],
                                 start=(kb == 0), stop=(kb == 15)),
                             reads=[w1k] + xk, writes=[("mehps", hb % 2)])
                    p.op("scalar", lambda e2, ps=ps: e2.activation(
                        sg[:], ps[:], AF.Silu),
                        reads=[("mehps", hb % 2)], writes=["mesg"])
                    p.op("vector", lambda e2, hb=hb: e2.tensor_tensor(
                        z_sb[:, hb * T:(hb + 1) * T], sg[:], crep[:],
                        ALU.mult),
                        reads=["mesg", ck], writes=[("mez", hb)])
                for dg in range(16):
                    po = ops_[dg % 2]
                    for hb in range(8):
                        p.op("tensor", lambda e2, e=e, dg=dg, hb=hb, po=po:
                             e2.matmul(
                                 po[:, 0:T],
                                 w2b[e % 2][:, (hb * 16 + dg) * 128:
                                            (hb * 16 + dg) * 128 + 128],
                                 z_sb[:, hb * T:(hb + 1) * T],
                                 start=(hb == 0), stop=(hb == 7)),
                             reads=[w2k, ("mez", hb)],
                             writes=[("meops", dg % 2)])
                    if e == 0:
                        p.op("vector", lambda e2, dg=dg, po=po: e2.tensor_copy(
                            acc[:, dg * T:(dg + 1) * T], po[:, 0:T]),
                            reads=[("meops", dg % 2)], writes=[("meacc", dg)])
                    else:
                        p.op("vector", lambda e2, dg=dg, po=po: e2.tensor_tensor(
                            acc[:, dg * T:(dg + 1) * T],
                            acc[:, dg * T:(dg + 1) * T], po[:, 0:T], ALU.add),
                            reads=[("meops", dg % 2), ("meacc", dg)],
                            writes=[("meacc", dg)])
            for dg in range(16):
                p.op("scalar", lambda e2, dg=dg: e2.activation(
                    sg[:], acc[:, dg * T:(dg + 1) * T], AF.Silu),
                    reads=[("meacc", dg)], writes=["mesg"])
                dma(p, "sync", g1T_dram[dg * 128:(dg + 1) * 128, t0:t0 + T],
                    sg[:], reads=["mesg"], writes=[("meg1o", tc, dg)])
            p.join_all()


def make_evac_dram(c, name, out_dram, width=NT, dtype=BF16, queue="gpsimd",
                   ntc=4):
    bufs = [c.sb(f"{name}_ev{i}", [128, width], dtype) for i in range(2)]
    p = c.p

    def evac(ob, t0, t1, ps, pskey):
        buf = bufs[ob % 2]
        bk = (name, "evb", ob % 2)
        p.op("vector", lambda e: e.tensor_copy(buf[:, t0:t1], ps[:, 0:t1 - t0]),
             reads=[pskey], writes=[bk])
        if t1 == width:
            dma(p, queue, out_dram[ob * 128:(ob + 1) * 128, :], buf[:],
                reads=[bk], writes=[(name, "out", ob)])

    return evac


def build_launch_L0a():
    nc = bass.Bass("TRN2", target_bir_lowering=False)
    x = nc.dram_tensor("x", [NT, D], F32, kind="ExternalInput")
    WinT = nc.dram_tensor("WinT", [D, D], BF16, kind="ExternalInput")
    WdtT = nc.dram_tensor("WdtT", [D, 32], BF16, kind="ExternalInput")
    bdt = nc.dram_tensor("bdt", [32, 1], F32, kind="ExternalInput")
    xT0 = nc.dram_tensor("xT0", [D, NT], F32, kind="ExternalOutput")
    xnT = nc.dram_tensor("xnT", [D, NT], BF16, kind="ExternalOutput")
    xpT = nc.dram_tensor("xpT", [D, NT], BF16, kind="ExternalOutput")
    dt = nc.dram_tensor("dt", [32, NT], F32, kind="ExternalOutput")
    stack = contextlib.ExitStack()
    with stack:
        c = Ctx(nc, stack)
        idf, idb = build_identity(c)
        for v in (LN_EPS, RMS_EPS, 1.0):
            c.constf(v)
        with c.scope():
            phase0(c, x.ap(), xT0.ap(), xnT.ap(), idf, idb)
        with c.scope():
            phase1(c, xnT.ap(), WinT, WdtT, bdt, xpT.ap(), dt)
        c.p.finish()
    return nc


def build_launch_L0b1():
    nc = bass.Bass("TRN2", target_bir_lowering=False)
    xpT = nc.dram_tensor("xpT", [D, NT], BF16, kind="ExternalInput")
    dt = nc.dram_tensor("dt", [32, NT], F32, kind="ExternalInput")
    logA = nc.dram_tensor("logA", [128, 16], F32, kind="ExternalInput")
    sumP = nc.dram_tensor("sumP", [128, 16], F32, kind="ExternalOutput")
    sumE = nc.dram_tensor("sumE", [128, 16], F32, kind="ExternalOutput")
    stack = contextlib.ExitStack()
    with stack:
        c = Ctx(nc, stack)
        with c.scope():
            xp_sb = c.sb("xp_sb", [128, 16 * NT], BF16)
            xp_keys = load_actT(c, "xp", xpT.ap(), xp_sb, 0)
            scan_passA(c, xp_sb, xp_keys, dt, logA, sumP.ap(), sumE.ap())
        c.p.finish()
    return nc


def build_launch_L0b2():
    nc = bass.Bass("TRN2", target_bir_lowering=False)
    xpT = nc.dram_tensor("xpT", [D, NT], BF16, kind="ExternalInput")
    dt = nc.dram_tensor("dt", [32, NT], F32, kind="ExternalInput")
    logA = nc.dram_tensor("logA", [128, 16], F32, kind="ExternalInput")
    priorP = nc.dram_tensor("priorP", [3 * 128, 16], F32, kind="ExternalInput")
    priorE = nc.dram_tensor("priorE", [3 * 128, 16], F32, kind="ExternalInput")
    WoutT = nc.dram_tensor("WoutT", [D, D], BF16, kind="ExternalInput")
    s0T = nc.dram_tensor("s0T", [D, NT], BF16, kind="ExternalOutput")
    stack = contextlib.ExitStack()
    with stack:
        c = Ctx(nc, stack)
        ys_sb = c.sb("ys_sb", [128, 16 * NT], BF16)
        yskeys = None
        with c.scope():
            xp_sb = c.sb("xp_sb", [128, 16 * NT], BF16)
            xp_keys = load_actT(c, "xp", xpT.ap(), xp_sb, 0)
            yskeys = scan_passB(c, xp_sb, xp_keys, dt, logA, priorP, priorE,
                                ys_sb)
            c.p.join_all()
        with c.scope():
            wbufs = [c.sb(f"wo_wt{i}", [128, 16 * 2 * 128], BF16)
                     for i in range(2)]
            pst = [c.ps(f"wo_ps{i}", [128, 512], F32) for i in range(4)]
            evac = make_evac_dram(c, "s0", s0T.ap())

            def act_view(kb, t0, t1):
                return ys_sb[:, kb * NT + t0: kb * NT + t1]

            mm_chain(c, "womm", WoutT.ap(), D, D, NT, act_view, yskeys, evac,
                     pst, wbufs)
            c.p.join_all()
        c.p.finish()
    return nc


def build_launch_FFN():
    nc = bass.Bass("TRN2", target_bir_lowering=False)
    xnT = nc.dram_tensor("xnT", [D, NT], BF16, kind="ExternalInput")
    WguT = nc.dram_tensor("WguT", [D, GUP], BF16, kind="ExternalInput")
    WdownT = nc.dram_tensor("WdownT", [I_FFN, D], BF16, kind="ExternalInput")
    g0T = nc.dram_tensor("g0T", [D, NT], BF16, kind="ExternalOutput")
    scr = nc.dram_tensor("ffn_scr", [2, NT], F32)
    stack = contextlib.ExitStack()
    with stack:
        c = Ctx(nc, stack)
        c.constf(RMS_EPS)
        phase_ffn(c, xnT.ap(), WguT.ap(), WdownT.ap(), g0T.ap(), scr)
        c.p.finish()
    return nc


def build_launch_GATE():
    nc = bass.Bass("TRN2", target_bir_lowering=False)
    xT = nc.dram_tensor("xT", [D, NT], F32, kind="ExternalInput")
    sT = nc.dram_tensor("sT", [D, NT], BF16, kind="ExternalInput")
    gT = nc.dram_tensor("gT", [D, NT], BF16, kind="ExternalInput")
    x1T = nc.dram_tensor("x1T", [D, NT], F32, kind="ExternalOutput")
    xn1T = nc.dram_tensor("xn1T", [D, NT], BF16, kind="ExternalOutput")
    xn1T32 = nc.dram_tensor("xn1T32", [D, NT], F32, kind="ExternalOutput")
    scr = nc.dram_tensor("gate_scr", [2, NT], F32)
    stack = contextlib.ExitStack()
    with stack:
        c = Ctx(nc, stack)
        c.constf(LN_EPS)
        phase_gate_ln(c, xT.ap(), sT.ap(), gT.ap(), x1T.ap(), xn1T.ap(), scr,
                      xn1T32.ap())
        c.p.finish()
    return nc


def build_launch_MOE():
    nc = bass.Bass("TRN2", target_bir_lowering=False)
    xn1T = nc.dram_tensor("xn1T", [D, NT], BF16, kind="ExternalInput")
    xn1T32 = nc.dram_tensor("xn1T32", [D, NT], F32, kind="ExternalInput")
    WgT = nc.dram_tensor("WgT", [D, 16], F32, kind="ExternalInput")
    bbias = nc.dram_tensor("bbias", [16, 1], F32, kind="ExternalInput")
    W1f = nc.dram_tensor("W1f", [NE * D, EHID], BF16, kind="ExternalInput")
    W2f = nc.dram_tensor("W2f", [NE * EHID, D], BF16, kind="ExternalInput")
    g1T = nc.dram_tensor("g1T", [D, NT], BF16, kind="ExternalOutput")
    comb = nc.dram_tensor("comb_scr", [16, NT], F32)
    stack = contextlib.ExitStack()
    with stack:
        c = Ctx(nc, stack)
        idf, idb = build_identity(c)
        phase_moe_gate(c, xn1T32.ap(), WgT.ap(), bbias.ap(), comb, idf)
        phase_moe_experts(c, xn1T.ap(), W1f.ap(), W2f.ap(), comb, g1T.ap())
        c.p.finish()
    return nc


def build_launch_QKV():
    nc = bass.Bass("TRN2", target_bir_lowering=False)
    xn1T = nc.dram_tensor("xn1T", [D, NT], BF16, kind="ExternalInput")
    WqkvT = nc.dram_tensor("WqkvT", [D, 3 * D], BF16, kind="ExternalInput")
    cosT = nc.dram_tensor("cosT", [128, NT], F32, kind="ExternalInput")
    sinN = nc.dram_tensor("sinN", [128, NT], F32, kind="ExternalInput")
    qr = nc.dram_tensor("qr", [D, NT], BF16, kind="ExternalOutput")
    kr = nc.dram_tensor("kr", [D, NT], BF16, kind="ExternalOutput")
    v = nc.dram_tensor("v", [D, NT], BF16, kind="ExternalOutput")
    stack = contextlib.ExitStack()
    with stack:
        c = Ctx(nc, stack)
        phase_qkv(c, xn1T.ap(), WqkvT.ap(), cosT.ap(), sinN.ap(),
                  qr.ap(), kr.ap(), v.ap())
        c.p.finish()
    return nc


def build_launch_ATTN():
    nc = bass.Bass("TRN2", target_bir_lowering=False)
    qa = nc.dram_tensor("qa", [D, NT], BF16, kind="ExternalInput")
    ka = nc.dram_tensor("ka", [D, NT], BF16, kind="ExternalInput")
    va = nc.dram_tensor("va", [D, NT], BF16, kind="ExternalInput")
    ya = nc.dram_tensor("ya", [2 * SEQ, 256], BF16, kind="ExternalOutput")
    scr = nc.dram_tensor("attn_scr", [1, 512], F32)
    stack = contextlib.ExitStack()
    with stack:
        c = Ctx(nc, stack)
        idf, idb = build_identity(c)
        c.constf(-20.0)
        phase_attn(c, qa, ka, va, ya.ap(), scr, idb)
        c.p.finish()
    return nc


def build_launch_WO():
    nc = bass.Bass("TRN2", target_bir_lowering=False)
    yb = nc.dram_tensor("yb", [2 * SEQ, 256], BF16, kind="ExternalInput")
    WoT = nc.dram_tensor("WoT", [D, D], BF16, kind="ExternalInput")
    s1T = nc.dram_tensor("s1T", [D, NT], BF16, kind="ExternalOutput")
    stack = contextlib.ExitStack()
    with stack:
        c = Ctx(nc, stack)
        phase_wo(c, yb, WoT.ap(), s1T.ap())
        c.p.finish()
    return nc


def build_launch_FINAL():
    nc = bass.Bass("TRN2", target_bir_lowering=False)
    x1T = nc.dram_tensor("x1T", [D, NT], F32, kind="ExternalInput")
    s1T = nc.dram_tensor("s1T", [D, NT], BF16, kind="ExternalInput")
    g1T = nc.dram_tensor("g1T", [D, NT], BF16, kind="ExternalInput")
    out = nc.dram_tensor("out", [NT, D], F32, kind="ExternalOutput")
    stack = contextlib.ExitStack()
    with stack:
        c = Ctx(nc, stack)
        idf, idb = build_identity(c)
        phase_final(c, x1T.ap(), s1T.ap(), g1T.ap(), out.ap(), idf)
        c.p.finish()
    return nc


def transpose_piece_to_bounce(c, src_dram, rows, cols, dst_dram, name,
                              reads=(), idb=None):
    p = c.p
    buf = c.sb(f"{name}_tb", [128, 2048], BF16)
    for ci in range(cols // 128):
        for r0 in range(0, rows, 2048):
            rw = min(2048, rows - r0)
            k = (name, "tb")
            p.op("sync", lambda e, ci=ci, r0=r0, rw=rw:
                 e.dma_start_transpose(
                     buf[:, 0:rw],
                     src_dram[r0:r0 + rw, ci * 128:ci * 128 + 128]),
                 reads=list(reads), writes=[k], dma=True)
            dma(p, "sync", dst_dram[ci * 128:ci * 128 + 128, r0:r0 + rw],
                buf[:, 0:rw], reads=[k], writes=[(name, "tout")])
    tail = cols % 128
    if tail:
        c0 = cols - tail
        tps = c.ps(f"{name}_tps", [128, 128], F32)
        tin = c.sb(f"{name}_tin", [128, tail], BF16)
        tout = c.sb(f"{name}_tout", [tail, 128], BF16)
        for rb in range(rows // 128):
            dma(p, "scalar", tin[:], src_dram[rb * 128:(rb + 1) * 128,
                                              c0:cols],
                reads=list(reads), writes=[(name, "tin")])
            p.op("tensor", lambda e: e.matmul(
                tps[0:tail, 0:128], tin[:], idb[:], start=True, stop=True),
                reads=[(name, "tin"), "identb"], writes=[(name, "tps")])
            p.op("vector", lambda e: e.tensor_copy(tout[:], tps[0:tail, 0:128]),
                 reads=[(name, "tps")], writes=[(name, "tev")])
            dma(p, "scalar", dst_dram[c0:cols, rb * 128:(rb + 1) * 128],
                tout[:], reads=[(name, "tev")], writes=[(name, "tout")])


def fused_scan(c, xpT_dram, dt_dram, logA_dram, selhot_dram, WoutT_ap,
               wout_dep, s0T_dram, summ_bounce, summ_gath):
    p = c.p
    nc = c.nc
    ys_sb = c.sb("fs_ys", [128, 16 * NT], BF16)
    with c.scope():
        xp_sb = c.sb("fs_xp", [128, 16 * NT], BF16)
        xp_keys = load_actT(c, "fsxp", xpT_dram, xp_sb, 0)
        scan_passA(c, xp_sb, xp_keys, dt_dram, logA_dram,
                   summ_bounce.ap()[:, 0:16], summ_bounce.ap()[:, 16:32])
        tok = p.collective("AllGather", ALU.bypass,
                           [[0, 1, 2, 3], [4, 5, 6, 7]],
                           summ_bounce.ap(), summ_gath.ap(),
                           reads=["summ_out"], writes=["summ_g"])
        sel = c.sb("fs_sel", [128, 4], F32)
        dma(p, "scalar", sel[:], selhot_dram.ap(), reads=[], writes=["fssel"])
        summ = c.sb("fs_summ", [128, 4 * 32], F32)
        p.wait_token(tok, engines=("scalar",))
        for k in range(4):
            dma(p, "scalar", summ[:, k * 32:(k + 1) * 32],
                summ_gath.ap()[k * 128:(k + 1) * 128, :],
                reads=["summ_g"], writes=["fssumm"])
        carry = c.sb("fs_carry", [128, 16], F32)
        car = c.sb("fs_car", [128, 16], F32)
        p.op("gpsimd", lambda e: e.memset(car[:], 0.0), writes=["fscar"])
        p.op("gpsimd", lambda e: e.memset(carry[:], 0.0), writes=["fscarry"])
        for k in range(3):
            p.op("vector", lambda e, k=k: e.tensor_tensor(
                car[:], summ[:, k * 32:k * 32 + 16], car[:], ALU.mult),
                reads=["fssumm", "fscar"], writes=["fscar"])
            p.op("vector", lambda e, k=k: e.tensor_tensor(
                car[:], car[:], summ[:, k * 32 + 16:k * 32 + 32], ALU.add),
                reads=["fscar", "fssumm"], writes=["fscar"])
            p.op("vector", lambda e, k=k: e.tensor_scalar(
                summ[:, 96:112], car[:], sel[:, k + 1:k + 2], None, ALU.mult),
                reads=["fscar", "fssel"], writes=["fstmp"])
            p.op("vector", lambda e: e.tensor_tensor(
                carry[:], carry[:], summ[:, 96:112], ALU.add),
                reads=["fscarry", "fstmp"], writes=["fscarry"])
        acols = load_acols(c, logA_dram)
        abufs = [c.sb(f"fs_a{i}", [128, NT], F32) for i in range(2)]
        yskeys = []
        for g in range(16):
            ak = build_a_group(c, g, dt_dram, abufs[g % 2], acols, g % 2)
            kk = ("fsys", g)
            p.op("vector", lambda e, g=g: e.tensor_tensor_scan(
                ys_sb[:, g * NT:(g + 1) * NT], abufs[g % 2][:],
                xp_sb[:, g * NT:(g + 1) * NT], carry[:, g:g + 1],
                ALU.mult, ALU.add),
                reads=[ak, xp_keys[g], "fscarry"], writes=[kk])
            yskeys.append(kk)
        p.join_all()
    with c.scope():
        wbufs = [c.sb(f"fs_wt{i}", [128, 16 * 2 * 128], BF16) for i in range(2)]
        pst = [c.ps(f"fs_ps{i}", [128, 512], F32) for i in range(4)]
        evac = make_evac_dram(c, "fs_s0", s0T_dram)

        def act_view(kb, a, b):
            return ys_sb[:, kb * NT + a: kb * NT + b]

        mm_chain(c, "fswo", WoutT_ap, D, D, NT, act_view, yskeys, evac,
                 pst, wbufs, wt_deps=[wout_dep])
        p.join_all()


def emit_cast_d2d(c, src_dram, dst_dram, rows, cols, name, nchunk=4):
    p = c.p
    step = cdiv(rows // 128, nchunk) * 128
    i = 0
    for r0 in range(0, rows, step):
        r1 = min(r0 + step, rows)
        dma(p, "gpsimd", dst_dram[r0:r1, :], src_dram[r0:r1, :],
            reads=[], writes=[(name, "out", i)])
        i += 1
    return (name, "out", i - 1)


def emit_dequant_T(c, pk_dram, scE_dram, n_out, W, outT_dram, name, idb, tps,
                   GT=16):
    p = c.p
    LN2 = float(np.log(2.0))
    WC = 32
    M = 8 * WC
    n_in = 8 * W
    nch = cdiv(n_in, 128)
    pk = c.sb(f"{name}_pk", [128, W], I32)
    sh = c.sb(f"{name}_sh", [128, WC], I32)
    nb = c.sb(f"{name}_nb", [128, M], I32)
    mi = c.sb(f"{name}_mi", [128, M], I32)
    nf = c.sb(f"{name}_nf", [128, M], F32)
    m8 = c.sb(f"{name}_m8", [128, M], F32)
    em = c.sb(f"{name}_em", [128, M], F32)
    mm = c.sb(f"{name}_mm", [128, M], F32)
    e2 = c.sb(f"{name}_e2", [128, M], F32)
    cf = c.sb(f"{name}_cf", [128, M], F32)
    val = c.sb(f"{name}_val", [128, M], F32)
    sc = c.sb(f"{name}_sc", [128, M], F32)
    ob = c.sb(f"{name}_ob", [128, M], BF16)
    stg = [c.sb(f"{name}_stg{ci}", [128, GT * 128], BF16)
           for ci in range(nch)]

    def v(fn, rd, wr):
        p.op("vector", fn, reads=rd, writes=wr)

    ntile = n_out // 128
    for ti in range(ntile):
        r0 = ti * 128
        gslot = ti % GT
        dma(p, "gpsimd", pk[:], pk_dram[r0:r0 + 128, :], reads=[],
            writes=["dpk"])
        for w0 in range(0, W, WC):
            w1 = min(w0 + WC, W)
            wn = w1 - w0
            mn = 8 * wn
            dma(p, "gpsimd", sc[:, 0:mn],
                scE_dram[r0:r0 + 128, 8 * w0:8 * w1], reads=[], writes=["dsc"])
            for k in range(8):
                v(lambda e, k=k, w0=w0, wn=wn: e.tensor_single_scalar(
                    sh[:, 0:wn], pk[:, w0:w0 + wn], 4 * k,
                    ALU.logical_shift_right), ["dpk"], ["dsh"])
                v(lambda e, k=k, wn=wn: e.tensor_single_scalar(
                    bass.AP(nb, k, [[M, 128], [8, wn]]), sh[:, 0:wn], 15,
                    ALU.bitwise_and), ["dsh"], ["dnb"])
            v(lambda e, mn=mn: e.tensor_single_scalar(
                mi[:, 0:mn], nb[:, 0:mn], 1, ALU.bitwise_and),
              ["dnb"], ["dmi"])
            v(lambda e, mn=mn: e.tensor_copy(mm[:, 0:mn], mi[:, 0:mn]),
              ["dmi"], ["dmm"])
            v(lambda e, mn=mn: e.tensor_single_scalar(
                mi[:, 0:mn], nb[:, 0:mn], 3, ALU.logical_shift_right),
              ["dnb", "dmi"], ["dmi"])
            v(lambda e, mn=mn: e.tensor_copy(m8[:, 0:mn], mi[:, 0:mn]),
              ["dmi"], ["dm8"])
            v(lambda e, mn=mn: e.tensor_copy(nf[:, 0:mn], nb[:, 0:mn]),
              ["dnb"], ["dnf"])
            v(lambda e, mn=mn: e.scalar_tensor_tensor(
                em[:, 0:mn], m8[:, 0:mn], -8.0, nf[:, 0:mn],
                ALU.mult, ALU.add), ["dm8", "dnf"], ["dem"])
            v(lambda e, mn=mn: e.tensor_tensor(
                e2[:, 0:mn], em[:, 0:mn], mm[:, 0:mn], ALU.subtract),
              ["dem", "dmm"], ["de2"])
            p.op("scalar", lambda e, mn=mn: e.activation(
                e2[:, 0:mn], e2[:, 0:mn], AF.Exp, scale=LN2 / 2),
                 reads=["de2"], writes=["de2"])
            v(lambda e, mn=mn: e.tensor_scalar(
                cf[:, 0:mn], mm[:, 0:mn], 0.25, 0.5, ALU.mult, ALU.add),
              ["dmm"], ["dcf"])
            v(lambda e, mn=mn: e.tensor_tensor(
                cf[:, 0:mn], cf[:, 0:mn], e2[:, 0:mn], ALU.mult),
              ["dcf", "de2"], ["dcf"])
            v(lambda e, mn=mn: e.tensor_single_scalar(
                val[:, 0:mn], em[:, 0:mn], 2.0, ALU.is_lt), ["dem"], ["dval"])
            v(lambda e, mn=mn: e.tensor_scalar(
                em[:, 0:mn], em[:, 0:mn], 0.5, None, ALU.mult),
              ["dem"], ["dem"])
            v(lambda e, mn=mn: e.tensor_tensor(
                em[:, 0:mn], em[:, 0:mn], cf[:, 0:mn], ALU.subtract),
              ["dem", "dcf"], ["dem"])
            v(lambda e, mn=mn: e.tensor_tensor(
                val[:, 0:mn], val[:, 0:mn], em[:, 0:mn], ALU.mult),
              ["dval", "dem"], ["dval"])
            v(lambda e, mn=mn: e.tensor_tensor(
                val[:, 0:mn], val[:, 0:mn], cf[:, 0:mn], ALU.add),
              ["dval", "dcf"], ["dval"])
            v(lambda e, mn=mn: e.tensor_scalar(
                m8[:, 0:mn], m8[:, 0:mn], -2.0, 1.0, ALU.mult, ALU.add),
              ["dm8"], ["dm8"])
            v(lambda e, mn=mn: e.tensor_tensor(
                val[:, 0:mn], val[:, 0:mn], m8[:, 0:mn], ALU.mult),
              ["dval", "dm8"], ["dval"])
            v(lambda e, mn=mn: e.tensor_tensor(
                ob[:, 0:mn], val[:, 0:mn], sc[:, 0:mn], ALU.mult),
              ["dval", "dsc"], ["dob"])
            for cb0 in range(8 * w0, 8 * w1, 128):
                wch = min(128, 8 * w1 - cb0)
                ci = cb0 // 128
                ps = tps[ci % 2]
                psk = (name, "tps", ci % 2)
                p.op("tensor", lambda e, cb0=cb0, w0=w0, wch=wch, ps=ps:
                     e.matmul(ps[0:wch, 0:128],
                              ob[:, cb0 - 8 * w0: cb0 - 8 * w0 + wch],
                              idb[:], start=True, stop=True),
                     reads=["dob", "identb"], writes=[psk])
                p.op("vector", lambda e, ci=ci, gslot=gslot, wch=wch, ps=ps:
                     e.tensor_copy(
                         stg[ci][0:wch, gslot * 128:(gslot + 1) * 128],
                         ps[0:wch, 0:128]),
                     reads=[psk], writes=[(name, "stg", ci)])
        if gslot == GT - 1 or ti == ntile - 1:
            g0 = (ti // GT) * GT
            gw = (ti - g0 + 1) * 128
            for ci in range(nch):
                wch = min(128, n_in - ci * 128)
                dma(p, "gpsimd",
                    outT_dram[ci * 128: ci * 128 + wch,
                              g0 * 128: g0 * 128 + gw],
                    stg[ci][0:wch, 0:gw],
                    reads=[(name, "stg", ci)], writes=[(name, "tout")])
    return (name, "tout")


def phase_attn2(c, qg_dram, kg_dram, vg_dram, ikq_dram, ya_fm, idb, toks):
    p = c.p
    EXPC = 20.0
    ikq = c.sb("a_ikq", [128, 16], mybir.dt.uint32)
    dma(p, "gpsimd", ikq[:], ikq_dram.ap(), reads=[], writes=["aikq"])
    for tok in toks:
        p.wait_token(tok, engines=("sync", "scalar", "gpsimd"))
    flats = {}
    for nm, t in (("q", qg_dram), ("k", kg_dram), ("v", vg_dram)):
        flats[nm] = bass.AP(t, 0, [[NT, 8 * D], [1, NT]])
    for b in range(2):
        for hsel in range(2):
            with c.scope():
                Ksb = c.sb("a_K", [128, 4 * NT], BF16)
                Qsb = c.sb("a_Q", [128, 4 * NT], BF16)
                Vsb = c.sb("a_V", [128, 64 * 128], BF16)
                vrow = c.sb("a_vr", [128, NT], BF16)
                sps = [c.ps(f"a_s{i}", [128, 512], F32) for i in range(2)]
                ypsd = [c.ps(f"a_y{i}", [128, 512], F32) for i in range(2)]
                dpsd = [c.ps(f"a_d{i}", [1, 512], F32) for i in range(2)]
                rps = c.ps("a_r", [128, 512], F32)
                kq_keys = []
                for i in range(4):
                    col = (b * 2 + hsel) * 4 + i
                    kk = ("aK", i)
                    p.op("gpsimd", lambda e, i=i, col=col: e.indirect_dma_start(
                        out=Ksb[:, i * NT:(i + 1) * NT], out_offset=None,
                        in_=flats["k"],
                        in_offset=bass.IndirectOffsetOnAxis(
                            ap=ikq[:, col:col + 1], axis=0),
                        element_offset=0),
                        reads=["aikq"], writes=[kk], dma=True)
                    qk = ("aQ", i)
                    p.op("gpsimd", lambda e, i=i, col=col: e.indirect_dma_start(
                        out=Qsb[:, i * NT:(i + 1) * NT], out_offset=None,
                        in_=flats["q"],
                        in_offset=bass.IndirectOffsetOnAxis(
                            ap=ikq[:, col:col + 1], axis=0),
                        element_offset=0),
                        reads=["aikq"], writes=[qk], dma=True)
                    kq_keys += [kk, qk]
                    vrk = ("avr",)
                    p.op("gpsimd", lambda e, i=i, col=col: e.indirect_dma_start(
                        out=vrow[:], out_offset=None, in_=flats["v"],
                        in_offset=bass.IndirectOffsetOnAxis(
                            ap=ikq[:, col:col + 1], axis=0),
                        element_offset=0),
                        reads=["aikq"], writes=[vrk], dma=True)
                    for pb in range(16):
                        ps = sps[pb % 2]
                        psk = ("asps", pb % 2)
                        p.op("tensor", lambda e, pb=pb, ps=ps: e.matmul(
                            ps[:, 0:128], vrow[:, pb * 128:(pb + 1) * 128],
                            idb[:], start=True, stop=True),
                            reads=[vrk, "identb"], writes=[psk])
                        vk = ("aV", i, pb)
                        p.op("vector", lambda e, i=i, pb=pb, ps=ps:
                             e.tensor_copy(
                                 Vsb[:, (i * 16 + pb) * 128:
                                     (i * 16 + pb) * 128 + 128],
                                 ps[:, 0:128]),
                             reads=[psk], writes=[vk])
                        kq_keys.append(vk)
                scat = [c.sb(f"a_sc{i}", [128, 2048], F32) for i in range(2)]
                phat = c.sb("a_ph", [128, 2 * 2048], BF16)
                ones_col = c.sb("a_ones", [128, 1], BF16)
                ones1 = c.sb("a_o1", [1, 128], F32)
                p.op("gpsimd", lambda e: e.memset(ones_col[:], 1.0),
                     writes=["aones"])
                p.op("gpsimd", lambda e: e.memset(ones1[:], 1.0),
                     writes=["aones1"])
                cm20, cm20k = c.constf(-EXPC)
                dmask = c.sb("a_dm", [128, 4 * 512], BF16)
                dmi = c.sb("a_dmi", [128, 4 * 512], F32)
                p.op("gpsimd", lambda e: e.iota(
                    dmi[:].rearrange("p (d q) -> p d q", d=4),
                    [[-128, 4], [1, 512]], base=0, channel_multiplier=-1,
                    allow_small_or_imprecise_dtypes=True),
                    writes=["admi"])
                p.op("vector", lambda e: e.tensor_single_scalar(
                    dmask[:], dmi[:], 0.0, ALU.is_ge),
                    reads=["admi"], writes=["adm"])
                ydiv = c.sb("a_yd", [128, 512], BF16)
                rrep = c.sb("a_rr", [128, 512], F32)
                drow = c.sb("a_dr", [1, 512], F32)
                for qc in range(16):
                    nkb = (qc + 1) * 4
                    ngrp = qc + 1
                    q0 = qc * 512
                    yps = ypsd[qc % 2]
                    ypk = ("ayps", qc % 2)
                    dps = dpsd[qc % 2]
                    dpk = ("adps", qc % 2)

                    def qk_group(g):
                        for hh in range(4):
                            kb = g * 4 + hh
                            ps = sps[kb % 2]
                            psk = ("asps", kb % 2)
                            p.op("tensor",
                                 lambda e, kb=kb, ps=ps: e.matmul(
                                     ps[:], Ksb[:, kb * 128:(kb + 1) * 128],
                                     Qsb[:, q0:q0 + 512],
                                     start=True, stop=True),
                                 reads=kq_keys, writes=[psk])
                            p.op("vector",
                                 lambda e, g=g, hh=hh, ps=ps: e.tensor_copy(
                                     scat[g % 2][:, hh * 512:(hh + 1) * 512],
                                     ps[:]),
                                 reads=[psk], writes=[("ascat", g % 2)])

                    def exp_group(g):
                        p.op("scalar", lambda e, g=g: e.activation(
                            phat[:, (g % 2) * 2048:(g % 2 + 1) * 2048],
                            scat[g % 2][:], AF.Exp,
                            bias=cm20[:, 0:1], scale=float(ADH) ** -0.5),
                            reads=[("ascat", g % 2), cm20k],
                            writes=[("aphat", g % 2)])

                    def consume_group(g):
                        for hh in range(4):
                            kb = g * 4 + hh
                            sl0 = (g % 2) * 2048 + hh * 512
                            if kb >= qc * 4:
                                d = kb - qc * 4
                                p.op("vector", lambda e, sl0=sl0, d=d:
                                     e.tensor_tensor(
                                         phat[:, sl0:sl0 + 512],
                                         phat[:, sl0:sl0 + 512],
                                         dmask[:, d * 512:(d + 1) * 512],
                                         ALU.mult),
                                     reads=[("aphat", g % 2), "adm"],
                                     writes=[("aphat", g % 2)])
                            p.op("tensor", lambda e, kb=kb, sl0=sl0: e.matmul(
                                dps[0:1, :], ones_col[:],
                                phat[:, sl0:sl0 + 512],
                                start=(kb == 0), stop=(kb == nkb - 1)),
                                reads=[("aphat", g % 2), "aones"],
                                writes=[dpk])
                            p.op("tensor", lambda e, kb=kb, sl0=sl0: e.matmul(
                                yps[:], Vsb[:, kb * 128:(kb + 1) * 128],
                                phat[:, sl0:sl0 + 512],
                                start=(kb == 0), stop=(kb == nkb - 1)),
                                reads=[("aphat", g % 2)] + kq_keys,
                                writes=[ypk])

                    qk_group(0)
                    for g in range(1, ngrp):
                        qk_group(g)
                        exp_group(g - 1)
                        consume_group(g - 1)
                    exp_group(ngrp - 1)
                    consume_group(ngrp - 1)
                    p.op("vector", lambda e, dps=dps, dpk=dpk: e.tensor_copy(
                        drow[:], dps[0:1, :]), reads=[dpk], writes=["adrow"])
                    p.op("vector", lambda e: e.reciprocal(drow[:], drow[:]),
                         reads=["adrow"], writes=["adrow"])
                    p.op("tensor", lambda e: e.matmul(
                        rps[:], ones1[:], drow[:], start=True, stop=True),
                        reads=["adrow", "aones1"], writes=["arps"])
                    p.op("vector", lambda e: e.tensor_copy(rrep[:], rps[:]),
                         reads=["arps"], writes=["arrep"])
                    p.op("vector", lambda e, yps=yps, ypk=ypk: e.tensor_tensor(
                        ydiv[:], yps[:], rrep[:], ALU.mult),
                        reads=[ypk, "arrep"], writes=["aydiv"])
                    dma(p, "sync",
                        ya_fm[hsel * 128:(hsel + 1) * 128,
                              b * SEQ + q0: b * SEQ + q0 + 512],
                        ydiv[:], reads=["aydiv"],
                        writes=[("ayout", b, hsel, qc)])
                p.join_all()


def phase_wo2(c, yag_dram, WoT_dram, s1T_dram, iwo_dram, wt_dep, yatok):
    p = c.p
    with c.scope():
        act_sb = c.sb("wo_act", [128, 16 * NT], BF16)
        iwo = c.sb("wo_idx", [128, 16], mybir.dt.uint32)
        dma(p, "gpsimd", iwo[:], iwo_dram.ap(), reads=[], writes=["woidx"])
        p.wait_token(yatok, engines=("scalar", "sync", "gpsimd"))
        flat = bass.AP(yag_dram, 0, [[NT, 2048 * 8], [1, NT]])
        act_keys = []
        for g in range(16):
            k = ("woact", g)
            p.op("gpsimd", lambda e, g=g: e.indirect_dma_start(
                out=act_sb[:, g * NT:(g + 1) * NT], out_offset=None,
                in_=flat,
                in_offset=bass.IndirectOffsetOnAxis(ap=iwo[:, g:g + 1],
                                                    axis=0),
                element_offset=0),
                reads=["woidx"], writes=[k], dma=True)
            act_keys.append(k)
        wbufs = [c.sb(f"wo_wt{i}", [128, 16 * 2 * 128], BF16) for i in range(2)]
        pst = [c.ps(f"wo_ps{i}", [128, 512], F32) for i in range(4)]
        evac = make_evac_dram(c, "s1", s1T_dram)

        def act_view(kb, a, bnd):
            return act_sb[:, kb * NT + a: kb * NT + bnd]

        mm_chain(c, "womm", WoT_dram, D, D, NT, act_view, act_keys, evac,
                 pst, wbufs, wt_deps=[wt_dep])
        p.join_all()


def fused_gather_reassemble(c, gath_dram, loc_dram, idx_dram, nsub, subw,
                            name):
    p = c.p
    idx = c.sb(f"{name}_idx", [128, nsub], mybir.dt.uint32)
    dma(p, "gpsimd", idx[:], idx_dram.ap(), reads=[], writes=[(name, "idx")])
    buf = c.sb(f"{name}_gb", [128, subw], BF16)
    total = gath_dram.shape[0] * gath_dram.shape[1]
    flat = bass.AP(gath_dram, 0, [[subw, total // subw], [1, subw]])
    for s in range(nsub):
        k = (name, "gb")
        p.op("gpsimd", lambda e, s=s: e.indirect_dma_start(
            out=buf[:], out_offset=None, in_=flat,
            in_offset=bass.IndirectOffsetOnAxis(ap=idx[:, s:s + 1], axis=0),
            element_offset=0),
            reads=[(name, "idx"), (name, "src")], writes=[k], dma=True)
        dst2 = bass.AP(loc_dram, s * 128 * subw, [[subw, 128], [1, subw]])
        dma(p, "gpsimd", dst2, buf[:], reads=[k], writes=[(name, "loc")])


def build_fused():
    nc = bass.Bass("TRN2", target_bir_lowering=False)
    x = nc.dram_tensor("x", [NT, D], F32, kind="ExternalInput")
    specs = [("WinT", D, D), ("WdtT", D, 32), ("WqkvT", D, 3 * D),
             ("WoutT", D, D), ("WoT", D, D),
             ("W1s", NE * D, EHID), ("W2s", NE * EHID, D)]
    ins, fulls = {}, {}
    for nm, r, cl in specs:
        ins[nm] = nc.dram_tensor(nm + "32", [r, cl], F32, kind="ExternalInput")
        fulls[nm] = nc.dram_tensor(nm + "bf", [r, cl], BF16)
    gup_pk = nc.dram_tensor("gup_pk", [GUP, 32], I32, kind="ExternalInput")
    gup_sc = nc.dram_tensor("gup_sc", [GUP, 256], F16, kind="ExternalInput")
    down_pk = nc.dram_tensor("down_pk", [D, 88], I32, kind="ExternalInput")
    down_sc = nc.dram_tensor("down_sc", [D, 704], F16, kind="ExternalInput")
    logA = nc.dram_tensor("logA", [128, 16], F32, kind="ExternalInput")
    bdt = nc.dram_tensor("bdt", [32, 1], F32, kind="ExternalInput")
    WgT32 = nc.dram_tensor("WgT32", [D, 16], F32, kind="ExternalInput")
    bbias = nc.dram_tensor("bbias", [16, 1], F32, kind="ExternalInput")
    cosT = nc.dram_tensor("cosT", [128, NT], F32, kind="ExternalInput")
    sinN = nc.dram_tensor("sinN", [128, NT], F32, kind="ExternalInput")
    selhot = nc.dram_tensor("selhot", [128, 4], F32, kind="ExternalInput")
    ikq = nc.dram_tensor("ikq", [128, 16], mybir.dt.uint32,
                         kind="ExternalInput")
    iwo = nc.dram_tensor("iwo", [128, 16], mybir.dt.uint32,
                         kind="ExternalInput")
    out = nc.dram_tensor("out", [NT, D], F32, kind="ExternalOutput")
    gupT_pc = nc.dram_tensor("gupT_pc", [256, GUP], BF16)
    downT_pc = nc.dram_tensor("downT_pc", [704, D], BF16)
    WguT_g = nc.dram_tensor("WguT_g", [D, GUP], BF16)
    WdownT_g = nc.dram_tensor("WdownT_g", [I_FFN, D], BF16)
    xT0 = nc.dram_tensor("xT0", [D, NT], F32)
    xnT = nc.dram_tensor("xnT", [D, NT], BF16,
        kind="ExternalOutput" if DBG_OUTS else "Internal")
    xpT = nc.dram_tensor("xpT", [D, NT], BF16,
        kind="ExternalOutput" if DBG_OUTS else "Internal")
    dtd = nc.dram_tensor("dtd", [32, NT], F32,
        kind="ExternalOutput" if DBG_OUTS else "Internal")
    summ_b = nc.dram_tensor("summ_b", [128, 32], F32)
    summ_g = nc.dram_tensor("summ_g", [512, 32], F32)
    s0T = nc.dram_tensor("s0T", [D, NT], BF16,
        kind="ExternalOutput" if DBG_OUTS else "Internal")
    g0T = nc.dram_tensor("g0T", [D, NT], BF16,
        kind="ExternalOutput" if DBG_OUTS else "Internal")
    x1T = nc.dram_tensor("x1T", [D, NT], F32,
        kind="ExternalOutput" if DBG_OUTS else "Internal")
    xn1T = nc.dram_tensor("xn1T", [D, NT], BF16,
        kind="ExternalOutput" if DBG_OUTS else "Internal")
    xn1T32 = nc.dram_tensor("xn1T32", [D, NT], F32)
    ffn_scr = nc.dram_tensor("ffn_scr", [2, NT], F32)
    gate_scr = nc.dram_tensor("gate_scr", [2, NT], F32)
    qr = nc.dram_tensor("qr", [D, NT], BF16)
    kr = nc.dram_tensor("kr", [D, NT], BF16)
    vv = nc.dram_tensor("vv", [D, NT], BF16)
    qr_g = nc.dram_tensor("qr_g", [8 * D, NT], BF16)
    kr_g = nc.dram_tensor("kr_g", [8 * D, NT], BF16)
    vv_g = nc.dram_tensor("vv_g", [8 * D, NT], BF16)
    ya_fm = nc.dram_tensor("ya_fm", [256, 2 * SEQ], BF16)
    ya_g = nc.dram_tensor("ya_g", [2048, 2 * SEQ], BF16)
    comb = nc.dram_tensor("comb_scr", [16, NT], F32)
    s1T = nc.dram_tensor("s1T", [D, NT], BF16,
        kind="ExternalOutput" if DBG_OUTS else "Internal")
    g1T = nc.dram_tensor("g1T", [D, NT], BF16,
        kind="ExternalOutput" if DBG_OUTS else "Internal")

    stack = contextlib.ExitStack()
    with stack:
        c = Ctx(nc, stack)
        p = c.p
        idf, idb = build_identity(c)
        for v in (LN_EPS, RMS_EPS, 1.0, -20.0):
            c.constf(v)
        AG8 = [list(range(8))]
        cdep = {}

        for nm in ("WinT", "WdtT", "WqkvT", "WoutT", "WoT"):
            r, cl = dict((s[0], (s[1], s[2])) for s in specs)[nm]
            cdep[nm] = emit_cast_d2d(c, ins[nm].ap(), fulls[nm].ap(), r, cl,
                                     nm, nchunk=2)

        with c.scope():
            phase0(c, x.ap(), xT0.ap(), xnT.ap(), idf, idb)

        for nm in ("W1s", "W2s"):
            r, cl = dict((s[0], (s[1], s[2])) for s in specs)[nm]
            cdep[nm] = emit_cast_d2d(c, ins[nm].ap(), fulls[nm].ap(), r, cl,
                                     nm, nchunk=4)

        with c.scope():
            phase1(c, xnT.ap(), fulls["WinT"], fulls["WdtT"], bdt, xpT.ap(),
                   dtd, windep=[cdep["WinT"]], wdtdep=[cdep["WdtT"]])

        dq_scope = c.scope()
        dq_scope.__enter__()
        dqtps = [c.ps(f"dq_tps{i}", [128, 128], F32) for i in range(2)]
        gu_dep = emit_dequant_T(c, gup_pk.ap(), gup_sc.ap(), GUP, 32,
                                gupT_pc.ap(), "dqg", idb, dqtps, GT=8)
        dn_dep = emit_dequant_T(c, down_pk.ap(), down_sc.ap(), D, 88,
                                downT_pc.ap(), "dqd", idb, dqtps, GT=4)
        gu_tok = p.collective("AllGather", ALU.bypass, AG8, gupT_pc.ap(),
                              WguT_g.ap(), reads=[gu_dep], writes=["wgu_g"])
        dn_tok = p.collective("AllGather", ALU.bypass, AG8, downT_pc.ap(),
                              WdownT_g.ap(), reads=[dn_dep], writes=["wdn_g"])

        with c.scope():
            fused_scan(c, xpT.ap(), dtd, logA, selhot, fulls["WoutT"].ap(),
                       cdep["WoutT"], s0T.ap(), summ_b, summ_g)
        dq_scope.__exit__(None, None, None)

        p.wait_token(gu_tok)
        p.wait_token(dn_tok)
        with c.scope():
            phase_ffn(c, xnT.ap(), WguT_g.ap(), WdownT_g.ap(), g0T.ap(),
                      ffn_scr)

        with c.scope():
            phase_gate_ln(c, xT0.ap(), s0T.ap(), g0T.ap(), x1T.ap(),
                          xn1T.ap(), gate_scr, xn1T32.ap())

        with c.scope():
            phase_moe_gate(c, xn1T32.ap(), WgT32.ap(), bbias.ap(), comb, idf)

        with c.scope():
            phase_qkv(c, xn1T.ap(), fulls["WqkvT"].ap(), cosT.ap(), sinN.ap(),
                      qr.ap(), kr.ap(), vv.ap(), wdep=[cdep["WqkvT"]])
        qtok = p.collective("AllGather", ALU.bypass, AG8, qr.ap(), qr_g.ap(),
                            reads=[("qout", ob) for ob in range(16)],
                            writes=[("qrg", "src")])
        ktok = p.collective("AllGather", ALU.bypass, AG8, kr.ap(), kr_g.ap(),
                            reads=[("qout", ob) for ob in range(16, 32)],
                            writes=[("krg", "src")])
        vtok = p.collective("AllGather", ALU.bypass, AG8, vv.ap(), vv_g.ap(),
                            reads=[("qout", ob) for ob in range(32, 48)],
                            writes=[("vvg", "src")])
        phase_moe_experts(c, xn1T.ap(), fulls["W1s"].ap(), fulls["W2s"].ap(),
                          comb, g1T.ap(), w1dep=[cdep["W1s"]],
                          w2dep=[cdep["W2s"]])

        phase_attn2(c, qr_g, kr_g, vv_g, ikq, ya_fm.ap(), idb,
                    [qtok, ktok, vtok])
        yatok = p.collective(
            "AllGather", ALU.bypass, AG8, ya_fm.ap(), ya_g.ap(),
            reads=[("ayout", b, hs, qc) for b in range(2) for hs in range(2)
                   for qc in range(16)],
            writes=[("yag", "src")])

        phase_wo2(c, ya_g, fulls["WoT"].ap(), s1T.ap(), iwo, cdep["WoT"],
                  yatok)

        with c.scope():
            phase_final(c, x1T.ap(), s1T.ap(), g1T.ap(), out.ap(), idf)
        p.finish()
    return nc


def _shard_tokens(x):
    out = []
    for core in range(8):
        b, cch = core // 4, core % 4
        out.append(np.ascontiguousarray(x[b, cch * NT:(cch + 1) * NT, :]))
    return out


def _run(nc, in_maps):
    res = run_bass_kernel_spmd(nc, in_maps, core_ids=list(range(NCORES)))
    return res.results


NCORES = 8


def kernel_unfused(L0_attn_x, L0_attn_W_in, L0_attn_W_out, L0_attn_log_A,
           L0_attn_W_dt, L0_attn_b_dt, L0_ffn_gate_up_packed,
           L0_ffn_gate_up_scales, L0_ffn_down_packed, L0_ffn_down_scales,
           L1_attn_W_qkv, L1_attn_W_o, L1_ffn_Wg, L1_ffn_balance_bias,
           L1_ffn_W1, L1_ffn_W2):
    f32 = np.float32
    C = lambda a: np.ascontiguousarray(a)
    x = np.asarray(L0_attn_x, f32)
    shards = _shard_tokens(x)

    WinT = C(np.asarray(L0_attn_W_in, f32).T)
    WoutT = C(np.asarray(L0_attn_W_out, f32).T)
    WoT = C(np.asarray(L1_attn_W_o, f32).T)
    WdtT = C(np.asarray(L0_attn_W_dt, f32).T)
    perm = np.concatenate([np.arange(0, ADH, 2), np.arange(1, ADH, 2)])
    WqkvT = C(np.asarray(L1_attn_W_qkv, f32).T).copy()
    base0 = WqkvT.copy()
    for h in range(AH):
        for base in (0, D):
            cols = base + h * ADH + np.arange(ADH)
            WqkvT[:, cols] = base0[:, cols[perm]]
    W1flat = np.asarray(L1_ffn_W1, f32).reshape(NE * D, EHID)
    W2flat = np.asarray(L1_ffn_W2, f32).reshape(NE * EHID, D)
    gup_pk = np.asarray(L0_ffn_gate_up_packed, np.int32)
    down_pk = np.asarray(L0_ffn_down_packed, np.int32)
    gup_scE = np.repeat(np.asarray(L0_ffn_gate_up_scales).T, 128, axis=1)
    down_scE = np.repeat(np.asarray(L0_ffn_down_scales).T, 128, axis=1)
    logA_cols = C(np.asarray(L0_attn_log_A, f32).reshape(16, 128).T)
    bdt = np.asarray(L0_attn_b_dt, f32).reshape(32, 1)
    WgT = C(np.asarray(L1_ffn_Wg, f32).T)
    bbias = np.asarray(L1_ffn_balance_bias, f32).reshape(16, 1)
    inv = 1.0 / 10000.0 ** (np.arange(0, ADH, 2, dtype=f32) / ADH)

    def cos_sin(t0):
        pos = np.arange(t0, t0 + NT, dtype=f32)
        ang = pos[None, :] * inv[:, None]
        return (C(np.concatenate([np.cos(ang), np.cos(ang)], 0)),
                C(np.concatenate([-np.sin(ang), np.sin(ang)], 0)))

    KS = D // 8
    prep_maps = []
    for j in range(NCORES):
        prep_maps.append({
            "WinT32": C(WinT[j * KS:(j + 1) * KS]),
            "WoutT32": C(WoutT[j * KS:(j + 1) * KS]),
            "WoT32": C(WoT[j * KS:(j + 1) * KS]),
            "WqkvT32": C(WqkvT[j * KS:(j + 1) * KS]),
            "WdtT32": C(WdtT[j * KS:(j + 1) * KS]),
            "W1s32": C(W1flat[j * 4096:(j + 1) * 4096]),
            "W2s32": C(W2flat[j * 2048:(j + 1) * 2048]),
            "gup_pk": C(gup_pk[:, j * 32:(j + 1) * 32]),
            "gup_sc": C(gup_scE[:, j * 256:(j + 1) * 256]),
            "down_pk": C(down_pk[:, j * 88:(j + 1) * 88]),
            "down_sc": C(down_scE[:, j * 704:(j + 1) * 704]),
        })
    rp = _run(build_launch_PREP(), prep_maps)
    cat = lambda nm: np.concatenate([rp[j][nm] for j in range(NCORES)], 0)
    WinT_bf, WoutT_bf, WoT_bf = cat("WinTbf"), cat("WoutTbf"), cat("WoTbf")
    WqkvT_bf, WdtT_bf = cat("WqkvTbf"), cat("WdtTbf")
    W1f, W2f = cat("W1sbf"), cat("W2sbf")
    WguT = np.concatenate([C(rp[j]["gup_dq"].T) for j in range(NCORES)], 0)
    WdownT = np.concatenate([C(rp[j]["down_dq"].T) for j in range(NCORES)], 0)

    r0 = _run(build_launch_L0a(),
              [{"x": shards[j], "WinT": WinT_bf, "WdtT": WdtT_bf, "bdt": bdt}
               for j in range(NCORES)])
    r1 = _run(build_launch_L0b1(),
              [{"xpT": r0[j]["xpT"], "dt": r0[j]["dt"], "logA": logA_cols}
               for j in range(NCORES)])
    maps2 = []
    for j in range(NCORES):
        b, cch = j // 4, j % 4
        P = np.ones((3 * 128, 16), f32)
        E = np.zeros((3 * 128, 16), f32)
        for k in range(cch):
            src = 4 * b + k
            P[(3 - cch + k) * 128:(4 - cch + k) * 128] = r1[src]["sumP"]
            E[(3 - cch + k) * 128:(4 - cch + k) * 128] = r1[src]["sumE"]
        maps2.append({"xpT": r0[j]["xpT"], "dt": r0[j]["dt"],
                      "logA": logA_cols, "priorP": P, "priorE": E,
                      "WoutT": WoutT_bf})
    r2 = _run(build_launch_L0b2(), maps2)
    r3 = _run(build_launch_FFN(),
              [{"xnT": r0[j]["xnT"], "WguT": WguT, "WdownT": WdownT}
               for j in range(NCORES)])
    r4 = _run(build_launch_GATE(),
              [{"xT": r0[j]["xT0"], "sT": r2[j]["s0T"], "gT": r3[j]["g0T"]}
               for j in range(NCORES)])

    qkv_maps = []
    for j in range(NCORES):
        cosT, sinN = cos_sin((j % 4) * NT)
        qkv_maps.append({"xn1T": r4[j]["xn1T"], "WqkvT": WqkvT_bf,
                         "cosT": cosT, "sinN": sinN})
    r5 = _run(build_launch_QKV(), qkv_maps)

    def a2a(nm):
        return [np.concatenate([r5[i][nm][j * 256:(j + 1) * 256]
                                for i in range(NCORES)], 0)
                for j in range(NCORES)]

    qa, ka, va = a2a("qr"), a2a("kr"), a2a("v")
    r6 = _run(build_launch_ATTN(),
              [{"qa": qa[j], "ka": ka[j], "va": va[j]} for j in range(NCORES)])
    ybs = [np.concatenate([r6[j]["ya"][i * NT:(i + 1) * NT]
                           for j in range(NCORES)], 0) for i in range(NCORES)]
    r7 = _run(build_launch_WO(),
              [{"yb": ybs[j], "WoT": WoT_bf} for j in range(NCORES)])
    r8 = _run(build_launch_MOE(),
              [{"xn1T": r4[j]["xn1T"], "xn1T32": r4[j]["xn1T32"], "WgT": WgT,
                "bbias": bbias, "W1f": W1f, "W2f": W2f}
               for j in range(NCORES)])
    r9 = _run(build_launch_FINAL(),
              [{"x1T": r4[j]["x1T"], "s1T": r7[j]["s1T"], "g1T": r8[j]["g1T"]}
               for j in range(NCORES)])

    out = np.empty((2, SEQ, D), f32)
    for j in range(NCORES):
        b, cch = j // 4, j % 4
        out[b, cch * NT:(cch + 1) * NT] = r9[j]["out"]
    return out


def kernel(L0_attn_x, L0_attn_W_in, L0_attn_W_out, L0_attn_log_A,
           L0_attn_W_dt, L0_attn_b_dt, L0_ffn_gate_up_packed,
           L0_ffn_gate_up_scales, L0_ffn_down_packed, L0_ffn_down_scales,
           L1_attn_W_qkv, L1_attn_W_o, L1_ffn_Wg, L1_ffn_balance_bias,
           L1_ffn_W1, L1_ffn_W2):
    f32 = np.float32
    C = lambda a: np.ascontiguousarray(a)
    x = np.asarray(L0_attn_x, f32)
    shards = _shard_tokens(x)

    WinT = C(np.asarray(L0_attn_W_in, f32).T)
    WoutT = C(np.asarray(L0_attn_W_out, f32).T)
    WoT = C(np.asarray(L1_attn_W_o, f32).T)
    WdtT = C(np.asarray(L0_attn_W_dt, f32).T)
    perm = np.concatenate([np.arange(0, ADH, 2), np.arange(1, ADH, 2)])
    WqkvT = C(np.asarray(L1_attn_W_qkv, f32).T).copy()
    base0 = WqkvT.copy()
    for h in range(AH):
        for base in (0, D):
            cols = base + h * ADH + np.arange(ADH)
            WqkvT[:, cols] = base0[:, cols[perm]]
    W1flat = C(np.asarray(L1_ffn_W1, f32).reshape(NE * D, EHID))
    W2flat = C(np.asarray(L1_ffn_W2, f32).reshape(NE * EHID, D))
    gup_pk = np.asarray(L0_ffn_gate_up_packed, np.int32)
    down_pk = np.asarray(L0_ffn_down_packed, np.int32)
    gup_scE = np.repeat(np.asarray(L0_ffn_gate_up_scales).T, 128, axis=1)
    down_scE = np.repeat(np.asarray(L0_ffn_down_scales).T, 128, axis=1)
    logA_cols = C(np.asarray(L0_attn_log_A, f32).reshape(16, 128).T)
    bdt = np.asarray(L0_attn_b_dt, f32).reshape(32, 1)
    WgT = C(np.asarray(L1_ffn_Wg, f32).T)
    bbias = np.asarray(L1_ffn_balance_bias, f32).reshape(16, 1)
    inv = 1.0 / 10000.0 ** (np.arange(0, ADH, 2, dtype=f32) / ADH)
    in_maps = []
    for j in range(NCORES):
        b, cch = j // 4, j % 4
        m = {"x": shards[j],
             "WinT32": WinT, "WoutT32": WoutT, "WoT32": WoT,
             "WqkvT32": WqkvT, "WdtT32": WdtT,
             "W1s32": W1flat, "W2s32": W2flat,
             "gup_pk": C(gup_pk[:, j * 32:(j + 1) * 32]),
             "gup_sc": C(gup_scE[:, j * 256:(j + 1) * 256]),
             "down_pk": C(down_pk[:, j * 88:(j + 1) * 88]),
             "down_sc": C(down_scE[:, j * 704:(j + 1) * 704]),
             "logA": logA_cols, "bdt": bdt, "WgT32": WgT, "bbias": bbias}
        pos = np.arange(cch * NT, (cch + 1) * NT, dtype=f32)
        ang = pos[None, :] * inv[:, None]
        m["cosT"] = C(np.concatenate([np.cos(ang), np.cos(ang)], 0))
        m["sinN"] = C(np.concatenate([-np.sin(ang), np.sin(ang)], 0))
        sh = np.zeros((128, 4), f32)
        sh[:, cch] = 1.0
        m["selhot"] = sh
        ikq = np.zeros((128, 16), np.uint32)
        for b2 in range(2):
            for hsel in range(2):
                for i in range(4):
                    col = (b2 * 2 + hsel) * 4 + i
                    ikq[:, col] = ((4 * b2 + i) * 2048 +
                                   (2 * j + hsel) * 128 + np.arange(128))
        m["ikq"] = ikq
        iwo = np.zeros((128, 16), np.uint32)
        wslot = b * 4 + cch
        for g in range(16):
            rows = (g // 2) * 256 + (g % 2) * 128 + np.arange(128)
            iwo[:, g] = rows * 8 + wslot
        m["iwo"] = iwo
        in_maps.append(m)

    res = _run(build_fused(), in_maps)
    out = np.empty((2, SEQ, D), f32)
    for j in range(NCORES):
        b, cch = j // 4, j % 4
        out[b, cch * NT:(cch + 1) * NT] = res[j]["out"]
    return out
